# version 24
# baseline (speedup 1.0000x reference)
import contextlib
import numpy as np

import concourse.bass as bass
import concourse.mybir as mybir
from concourse.bass_utils import run_bass_kernel_spmd

F32 = mybir.dt.float32
BF16 = mybir.dt.bfloat16
ALU = mybir.AluOpType
AF = mybir.ActivationFunctionType
AX = mybir.AxisListType

ENGS = ("sync", "scalar", "vector", "gpsimd", "tensor")
LN6 = float(np.log(np.float32(6.0)))
INV12 = float(np.float32(1.0 / 12.0))


class Prog:

    def __init__(self, nc):
        self.nc = nc
        self.q = {e: [] for e in ENGS}
        self.cnt = {}
        self.seen = {e: {} for e in ENGS}
        self.sems = {}
        self.fence_toks = []
        self.last_chained = {}

    def sem(self, es, name):
        self.sems[name] = es.enter_context(self.nc.semaphore(name))
        self.cnt[name] = 0

    def tok(self, sem):
        return (sem, self.cnt[sem])

    def fence(self):
        self.fence_toks = [(s, c) for s, c in self.cnt.items() if c > 0]

    OWN = {"vector": "DV", "scalar": "AC", "gpsimd": "PO"}

    def emit(self, eng, fn, deps=(), inc=None, inc_by=1, chain=True):
        w = []
        flat = []
        own = self.OWN.get(eng)
        if own is not None and self.cnt[own] > 0 and (chain or self.last_chained.get(eng, True)):
            flat.append((own, self.cnt[own]))
        self.last_chained[eng] = chain
        for d in list(deps) + self.fence_toks:
            if d is None:
                continue
            if isinstance(d, list):
                flat.extend(x for x in d if x is not None)
            else:
                flat.append(d)
        for d in flat:
            s, v = d
            if v <= 0 or self.seen[eng].get(s, 0) >= v:
                continue
            self.seen[eng][s] = v
            w.append((s, v))
        tok = None
        rec_inc = None
        if inc is not None:
            self.cnt[inc] += inc_by
            tok = (inc, self.cnt[inc])
            if eng not in self.OWN:
                self.seen[eng][inc] = self.cnt[inc]
            rec_inc = (inc, inc_by)
        self.q[eng].append((fn, w, rec_inc))
        return tok

    def check(self):
        ptr = {e: 0 for e in ENGS}
        cnt = {s: 0 for s in self.cnt}
        progress = True
        while progress:
            progress = False
            for e in ENGS:
                while ptr[e] < len(self.q[e]):
                    fn, waits, inc = self.q[e][ptr[e]]
                    if any(cnt[s] < v for s, v in waits):
                        break
                    if inc is not None:
                        cnt[inc[0]] += inc[1]
                    ptr[e] += 1
                    progress = True
        stuck = {e: (len(self.q[e]) - ptr[e]) for e in ENGS if ptr[e] < len(self.q[e])}
        if stuck:
            detail = {}
            for e in stuck:
                fn, waits, inc = self.q[e][ptr[e]]
                detail[e] = [(s, v, cnt[s]) for s, v in waits if cnt[s] < v]
            raise RuntimeError(f"deadlock: stuck ops {stuck}; heads wait {detail}")

    def realize(self, block):
        sems = self.sems

        def runner(eng_name):
            ops = self.q[eng_name]

            def run(eng):
                for fn, waits, inc in ops:
                    for s, v in waits:
                        eng.wait_ge(sems[s], v)
                    inst = fn(eng)
                    if inc is not None:
                        inst.then_inc(sems[inc[0]], inc[1])
            return run

        block.sync(runner("sync"))
        block.scalar(runner("scalar"))
        block.vector(runner("vector"))
        block.gpsimd(runner("gpsimd"))
        block.tensor(runner("tensor"))


def cdiv(a, b):
    return (a + b - 1) // b


def build_nc(S=2048, D=1024, I=2816, DBG=False, FFN_FENCE=False, LNQ_FENCE=False, EXT_KIND=False, DBGMASK=31):
    globals()["_FFN_FENCE"] = FFN_FENCE
    globals()["_LNQ_FENCE"] = LNQ_FENCE
    T = S // 128
    FC = D // 128
    IG = 512
    NG = cdiv(I, IG)
    QW = 1024

    nc = bass.Bass("TRN2", target_bir_lowering=False, detect_race_conditions=False)

    x_h = nc.dram_tensor("x", [S, D], F32, kind="ExternalInput").ap()
    wqkv_h = nc.dram_tensor("wqkv", [D, 3 * D], F32, kind="ExternalInput").ap()
    bqkv_h = nc.dram_tensor("bqkv", [3 * D], F32, kind="ExternalInput").ap()
    wout_h = nc.dram_tensor("wout", [D, D], F32, kind="ExternalInput").ap()
    bout_h = nc.dram_tensor("bout", [D], F32, kind="ExternalInput").ap()
    wg_h = nc.dram_tensor("wg", [I, D], F32, kind="ExternalInput").ap()
    wu_h = nc.dram_tensor("wu", [I, D], F32, kind="ExternalInput").ap()
    wd_h = nc.dram_tensor("wd", [D, I], F32, kind="ExternalInput").ap()
    y_h = nc.dram_tensor("y", [S, D], F32, kind="ExternalOutput").ap()
    wgT_d = nc.dram_tensor("wgT_d", [D, I], BF16, kind="ExternalOutput" if (DBG or EXT_KIND) else "Internal").ap()
    wuT_d = nc.dram_tensor("wuT_d", [D, I], BF16, kind="ExternalOutput" if (DBG or EXT_KIND) else "Internal").ap()
    wdT_d = nc.dram_tensor("wdT_d", [I, D], BF16, kind="ExternalOutput" if (DBG or EXT_KIND) else "Internal").ap()
    x1_d = nc.dram_tensor("x1_d", [S, D], F32, kind="ExternalOutput" if (DBG or EXT_KIND) else "Internal").ap()
    x2_d = nc.dram_tensor("x2_d", [S, D], F32, kind="ExternalOutput" if (DBG or EXT_KIND) else "Internal").ap()

    es = contextlib.ExitStack()

    def sbL(name, shape, dt):
        return es.enter_context(nc.sbuf_tensor(name, shape, dt, side="left"))

    def pst(name, shape, dt):
        return es.enter_context(nc.psum_tensor(name, shape, dt))

    p = Prog(nc)
    for s in ("PE", "DV", "AC", "PO"):
        p.sem(es, s)
    NLD, NST = 16, 8
    for i in range(NLD):
        p.sem(es, f"L{i}")
    for i in range(NST):
        p.sem(es, f"S{i}")
    rotor = {"L": 0, "S": 0}

    def st_fence_toks():
        return [(f"S{i}", p.cnt[f"S{i}"]) for i in range(NST)]

    def ld_fence_toks():
        return [(f"L{i}", p.cnt[f"L{i}"]) for i in range(NLD)]

    ident_b = sbL("ident_b", [128, 128], BF16)
    ident_f = sbL("ident_f", [128, 128], F32)
    bqkv_t = sbL("bqkv_t", [128, 3 * FC], F32)
    bout_t = sbL("bout_t", [128, FC], F32)
    c_ln6 = sbL("c_ln6", [128, 1], F32)
    c_eps = sbL("c_eps", [128, 1], F32)
    nc.const_aps.register(F32, LN6, c_ln6[:])
    nc.const_aps.register(F32, 1e-5, c_eps[:])
    MIDS = (0.25, 0.75, 1.25, 1.75, 2.5, 3.5, 5.0)
    c_mid = []
    for i, m in enumerate(MIDS):
        t = sbL(f"c_mid{i}", [128, 1], F32)
        nc.const_aps.register(F32, -float(m), t[:])
        c_mid.append(t)
    st_sum = sbL("st_sum", [128, T], F32)
    st_sq = sbL("st_sq", [128, T], F32)
    st_rstd = sbL("st_rstd", [128, T], F32)
    st_nbias = sbL("st_nbias", [128, T], F32)
    lnb_ref = [None]
    stat_dump = [None]
    stat_prev = [None]
    wq_vf = [None, None, None]

    class QSet:
        def __init__(self, i):
            self.i = i
            self.v = sbL(f"qs{i}_v", [128, QW], BF16)
            self.a = sbL(f"qs{i}_a", [128, QW], BF16)
            self.b = sbL(f"qs{i}_b", [128, QW], BF16)
            self.c = sbL(f"qs{i}_c", [128, QW], BF16)
            self.d = sbL(f"qs{i}_d", [128, QW], BF16)
            self.g = sbL(f"qs{i}_g", [128, QW], BF16)
            self.bm = sbL(f"qs{i}_bm", [128, QW // 16], F32)
            self.iv = sbL(f"qs{i}_iv", [128, QW // 16], F32)
            self.sh = sbL(f"qs{i}_sh", [128, QW // 16], F32)
            self.out = None
            self.ivtok = None
            self.ztok = None
    qsets = [QSet(0), QSet(1), QSet(2)]
    qrot = [0]
    wqrot = [0]
    sq_es = contextlib.ExitStack()

    rs1 = contextlib.ExitStack()

    def sbR(stk, name, shape, dt):
        return stk.enter_context(nc.sbuf_tensor(name, shape, dt, side="right"))

    x_res = sbR(rs1, "x_res", [128, T * D], F32)
    rs1b = contextlib.ExitStack()
    xq_fm = sbR(rs1b, "xq_fm", [128, T * D], BF16)
    lnb_ref[0] = sbR(rs1b, "lnb1", [128, 2 * D], BF16)
    rs1c = contextlib.ExitStack()
    wq_vf[0] = sbR(rs1c, "wqvf0", [128, QW], F32)
    wq_vf[1] = sbR(rs1c, "wqvf1", [128, QW], F32)
    wraw = sbR(rs1c, "wraw", [128, 2 * D], F32)
    wtm = [sbR(rs1c, "wtm0", [128, 2 * D], BF16),
           sbR(rs1c, "wtm1", [128, 2 * D], BF16),
           sbR(rs1c, "wtm2", [128, 2 * D], BF16)]
    IG1 = 256
    h_buf = [sbR(rs1c, "h_buf0", [128, T * IG1], BF16),
             sbR(rs1c, "h_buf1", [128, T * IG1], BF16)]
    wgt_g = [sbR(rs1c, "wgt_g0", [128, FC * IG1], BF16),
             sbR(rs1c, "wgt_g1", [128, FC * IG1], BF16)]
    wut_g = [sbR(rs1c, "wut_g0", [128, FC * IG1], BF16),
             sbR(rs1c, "wut_g1", [128, FC * IG1], BF16)]
    wdt_g = [sbR(rs1c, "wdt_g0", [128, 2 * D], BF16),
             sbR(rs1c, "wdt_g1", [128, 2 * D], BF16)]
    stat_dump[0] = wq_vf[0][:, 0:1024]
    braw = wq_vf[0][0:32, 0:128]


    ps_mm = [pst(f"ps_mm{i}", [128, 512], F32) for i in range(4)]
    ps_y = [pst(f"ps_y{i}", [128, 512], F32) for i in range(2)]
    ps_tr = [pst(f"ps_tr{i}", [128, 1024], BF16) for i in range(2)]

    tr_free = {0: None, 1: None}
    mm_free = {i: None for i in range(4)}
    y_free = {0: None, 1: None}

    def dma(out, in_, deps=(), sem="Ld"):
        pool, n = ("L", NLD) if sem == "Ld" else ("S", NST)
        s = f"{pool}{rotor[pool]}"
        rotor[pool] = (rotor[pool] + 1) % n
        prev = (s, p.cnt[s])
        return p.emit("sync", lambda e, o=out, i=in_: e.dma_start(out=o, in_=i),
                      deps=list(deps) + [prev], inc=s, inc_by=16)

    def transpose_batch(blocks, parity, deps=()):
        pt = ps_tr[parity]
        tok = None
        for k, src in enumerate(blocks):
            d = (list(deps) + [tr_free[parity]]) if k == 0 else ()
            tok = p.emit("tensor", lambda e, s=src, kk=k: e.transpose(
                pt[:, kk * 128:(kk + 1) * 128], s, ident_b[:]), deps=d, inc="PE")
        return tok

    def tr_evac(eng, parity, fn, deps=(), inc="DV"):
        tok = p.emit(eng, fn, deps=deps, inc=inc, chain=False)
        tr_free[parity] = tok
        return tok

    def tr_evac_act(parity, out_ap, w, deps=()):
        tok = p.emit("scalar", lambda e, o=out_ap, pr=parity, w_=w:
                     e.activation(out=o, in_=ps_tr[pr][:, 0:w_],
                                  func=AF.Identity),
                     deps=deps, inc="AC", chain=False)
        tr_free[parity] = tok
        return tok

    def quant_a(src_ap, dst_ap, width, deps, src_f32=False, act_levels=3,
                out_war=None, sh_scale=INV12):
        if src_f32:
            S = qsets[wqrot[0] % 2]
            wqrot[0] += 1
        else:
            S = qsets[qrot[0] % len(qsets)]
            qrot[0] += 1
        nb = width // 16
        v = (wq_vf[S.i] if src_f32 else S.v)[:, 0:width]
        bm, iv, sh = S.bm[:, 0:nb], S.iv[:, 0:nb], S.sh[:, 0:nb]
        g = S.g[:, 0:width]
        prev_out, prev_iv = S.out, S.ivtok
        deps = list(deps)
        t_ax = p.emit("scalar", lambda e: e.activation(
            out=v, in_=src_ap, func=AF.Abs), deps=deps + [prev_out],
            inc="AC", chain=False)
        v3 = v.rearrange("p (n b) -> p n b", b=16)
        p.emit("vector", lambda e: e.tensor_reduce(
            out=bm, in_=v3, axis=AX.X, op=ALU.max),
            deps=[t_ax, prev_iv, prev_out], inc="DV")
        t_bc = p.emit("vector", lambda e: e.tensor_scalar(
            out=bm, in0=bm, scalar1=6e-12, scalar2=None, op0=ALU.max), inc="DV")
        t_sh = p.emit("vector", lambda e: e.tensor_scalar(
            out=sh, in0=bm, scalar1=sh_scale, scalar2=None, op0=ALU.mult), inc="DV")
        p.emit("scalar", lambda e: e.activation(
            out=iv, in_=bm, func=AF.Ln), deps=[t_bc], inc="AC")
        t_iv = p.emit("scalar", lambda e: e.activation(
            out=iv, in_=iv, func=AF.Exp, scale=-1.0, bias=LN6), inc="AC")
        t_sg = p.emit("scalar", lambda e: e.activation(
            out=g, in_=src_ap, func=AF.Sign), deps=deps + [prev_out],
            inc="AC", chain=False)
        iv3 = iv.unsqueeze(2).broadcast_to([128, nb, 16])
        t_v = p.emit("gpsimd", lambda e: e.tensor_tensor(
            out=v3, in0=v3, in1=iv3, op=ALU.mult),
            deps=[t_iv, t_ax, prev_out], inc="PO", chain=False)
        g3 = g.rearrange("p (n b) -> p n b", b=16)
        sh3 = sh.unsqueeze(2).broadcast_to([128, nb, 16])
        t_ssg = p.emit("gpsimd", lambda e: e.tensor_tensor(
            out=g3, in0=g3, in1=sh3, op=ALU.mult), deps=[t_sg, t_sh],
            inc="PO", chain=False)
        S.ivtok = t_iv
        return dict(S=S, v=v, g=g, width=width, dst=dst_ap, t_v=t_v,
                    t_ssg=t_ssg, act_levels=act_levels, out_war=out_war,
                    t_ax=t_ax, t_sg=t_sg, ztok_war=S.ztok)

    def quant_b(st):
        S, v, g, width = st["S"], st["v"], st["g"], st["width"]
        A, B, C, Dd = (S.a[:, 0:width], S.b[:, 0:width], S.c[:, 0:width],
                       S.d[:, 0:width])
        t_v = st["t_v"]
        po = S.out
        qtok = st.pop("ztok_war")
        p.emit("vector", lambda e: e.tensor_scalar(
            out=A, in0=v, scalar1=0.25, scalar2=None, op0=ALU.is_gt),
            deps=[t_v], inc="DV", chain=False)
        for mm in (0.75, 1.25):
            p.emit("vector", lambda e, m_=mm: e.scalar_tensor_tensor(
                out=A, in0=v, scalar=m_, in1=A, op0=ALU.is_gt, op1=ALU.add),
                inc="DV", chain=False)
        t3 = p.emit("vector", lambda e: e.scalar_tensor_tensor(
            out=A, in0=v, scalar=1.75, in1=A, op0=ALU.is_gt, op1=ALU.add),
            inc="DV", chain=False)
        s5 = p.emit("scalar", lambda e: e.activation(
            out=B, in_=v, func=AF.Sign, bias=-2.5), deps=[t_v, qtok],
            inc="AC", chain=False)
        s6 = p.emit("scalar", lambda e: e.activation(
            out=C, in_=v, func=AF.Sign, bias=-3.5), deps=[po],
            inc="AC", chain=False)
        s7 = p.emit("scalar", lambda e: e.activation(
            out=Dd, in_=v, func=AF.Sign, bias=-5.0),
            inc="AC", chain=False)
        w1 = p.emit("vector", lambda e: e.tensor_tensor(
            out=B, in0=B, in1=C, op=ALU.add), deps=[s5, s6],
            inc="DV", chain=False)
        w2 = p.emit("vector", lambda e: e.scalar_tensor_tensor(
            out=B, in0=Dd, scalar=2.0, in1=B, op0=ALU.mult, op1=ALU.add),
            deps=[s7, w1], inc="DV", chain=False)
        t_qh = p.emit("vector", lambda e: e.scalar_tensor_tensor(
            out=C, in0=B, scalar=4.0, in1=A, op0=ALU.add, op1=ALU.add),
            deps=[w1, t3], inc="DV", chain=False)
        S.ztok = t_qh
        t_out = p.emit("gpsimd", lambda e, d_=st["dst"], C_=C, g_=g: e.tensor_tensor(
            out=d_, in0=C_, in1=g_, op=ALU.mult),
            deps=[t_qh, st["t_ssg"], st["out_war"]], inc="PO", chain=False)
        S.out = t_out
        return t_out

    def quant(src_ap, dst_ap, width, deps, src_f32=False, act_levels=3):
        return quant_b(quant_a(src_ap, dst_ap, width, deps, src_f32, act_levels))

    def ln_stats(deps, src_tile_ap, col):
        sd = stat_dump[0]
        p.emit("vector", lambda e: e.tensor_reduce(
            out=st_sum[:, col:col + 1], in_=src_tile_ap, axis=AX.X, op=ALU.add),
            deps=deps, inc="DV")
        t_sq = p.emit("scalar", lambda e, sd=sd: e.activation(
            out=sd, in_=src_tile_ap, func=AF.Square),
            deps=list(deps) + [stat_prev[0]], inc="AC", chain=False)
        t_red = p.emit("vector", lambda e, sd=sd: e.tensor_reduce(
            out=st_sq[:, col:col + 1], in_=sd, axis=AX.X, op=ALU.add),
            deps=[t_sq], inc="DV")
        stat_prev[0] = t_red
        return t_red

    def ln_finalize(deps, dbg=None):
        def dcp(slot, src, dep=None):
            if dbg is None:
                return None
            return p.emit("vector", lambda e: e.tensor_copy(
                out=dbg[:, slot * T:(slot + 1) * T], in_=src), deps=[dep] if dep else (), inc="DV")
        p.emit("vector", lambda e: e.tensor_scalar(
            out=st_sum[:], in0=st_sum[:], scalar1=1.0 / D, scalar2=None,
            op0=ALU.mult), deps=deps, inc="DV")
        p.emit("vector", lambda e: e.tensor_scalar(
            out=st_sq[:], in0=st_sq[:], scalar1=1.0 / D, scalar2=None,
            op0=ALU.mult), inc="DV")
        dcp(2, st_sq[:])
        p.emit("vector", lambda e: e.tensor_tensor(
            out=st_rstd[:], in0=st_sum[:], in1=st_sum[:], op=ALU.mult), inc="DV")
        dcp(3, st_rstd[:])
        t_var = p.emit("vector", lambda e: e.tensor_tensor(
            out=st_sq[:], in0=st_sq[:], in1=st_rstd[:], op=ALU.subtract), inc="DV")
        dcp(4, st_sq[:])
        t_ln = p.emit("scalar", lambda e: e.activation(
            out=st_rstd[:], in_=st_sq[:], func=AF.Ln, bias=1e-5),
            deps=[t_var], inc="AC")
        dcp(5, st_rstd[:], t_ln)
        t_r = p.emit("scalar", lambda e: e.activation(
            out=st_rstd[:], in_=st_rstd[:], func=AF.Exp, scale=-0.5),
            deps=[dbg and p.tok("DV")], inc="AC")
        dcp(6, st_rstd[:], t_r)
        return p.emit("vector", lambda e: e.scalar_tensor_tensor(
            out=st_nbias[:], in0=st_sum[:], scalar=-1.0, in1=st_rstd[:],
            op0=ALU.mult, op1=ALU.mult), deps=[t_r], inc="DV")

    p.emit("gpsimd", lambda e: e.memset(ident_b[:], 0.0), inc="PO")
    p.emit("gpsimd", lambda e: e.affine_select(
        out=ident_b[:], in_=ident_b[:], compare_op=ALU.not_equal, fill=1.0,
        base=0, pattern=[[-1, 128]], channel_multiplier=1), inc="PO")
    p.emit("gpsimd", lambda e: e.memset(ident_f[:], 0.0), inc="PO")
    t_idf = p.emit("gpsimd", lambda e: e.affine_select(
        out=ident_f[:], in_=ident_f[:], compare_op=ALU.not_equal, fill=1.0,
        base=0, pattern=[[-1, 128]], channel_multiplier=1), inc="PO")
    p.emit("gpsimd", lambda e: e.memset(c_ln6[:], LN6), inc="PO")
    p.emit("gpsimd", lambda e: e.memset(c_eps[:], 1e-5), inc="PO")
    for i, m in enumerate(MIDS):
        p.emit("gpsimd", lambda e, t=c_mid[i], mm=m: e.memset(t[:], -float(mm)), inc="PO")
    init_tok = p.tok("PO")
    p.fence()

    t_bq = dma(braw[0:3 * FC, :], bqkv_h.rearrange("(c q) -> c q", q=128))
    t_tbq = p.emit("tensor", lambda e: e.transpose(
        ps_y[0][:, 0:3 * FC], braw[0:3 * FC, :], ident_f[0:3 * FC, 0:3 * FC]),
        deps=[t_bq, t_idf], inc="PE")
    t_ebq = p.emit("vector", lambda e: e.tensor_copy(
        out=bqkv_t[:], in_=ps_y[0][:, 0:3 * FC]), deps=[t_tbq], inc="DV", chain=False)
    y_free[0] = t_ebq
    t_bo = dma(braw[0:FC, :], bout_h.rearrange("(c q) -> c q", q=128),
               deps=[t_ebq])
    t_tbo = p.emit("tensor", lambda e: e.transpose(
        ps_y[1][:, 0:FC], braw[0:FC, :], ident_f[0:FC, 0:FC]),
        deps=[t_bo], inc="PE")
    t_ebo = p.emit("vector", lambda e: e.tensor_copy(
        out=bout_t[:], in_=ps_y[1][:, 0:FC]), deps=[t_tbo], inc="DV", chain=False)
    y_free[1] = t_ebo
    bias_ready = t_ebo
    stat_prev[0] = t_ebo

    def lnq_phase(x_src_h, x_res, xq_fm, src_deps, dst_free, lnq_dbg=None):
        lnb = lnb_ref[0]
        """DMA x tiles into x_res, LayerNorm, fp4-quantize, transpose into
        xq_fm (tile-major fc blocks). Returns (x_toks, ready_token)."""
        x_toks = []
        for t in range(T):
            x_toks.append(dma(x_res[:, t * D:(t + 1) * D],
                              x_src_h[t * 128:(t + 1) * 128, :],
                              deps=[src_deps[t] if src_deps else None]))
        xq_t = [None] * T
        st_last = None
        for t in range(T):
            st_last = ln_stats([x_toks[t]], x_res[:, t * D:(t + 1) * D], t)
        if DBG and lnq_dbg:
            p.emit("vector", lambda e: e.tensor_copy(out=lnq_dbg[0][:, 0:T], in_=st_sum[:]),
                   deps=[st_last], inc="DV")
            st_last = p.emit("vector", lambda e: e.tensor_copy(out=lnq_dbg[0][:, T:2*T], in_=st_sq[:]),
                   deps=[st_last], inc="DV")
        t_nb = ln_finalize([st_last])
        if globals().get("_LNQ_FENCE"):
            p.fence()
        last = None
        lnb_tr = {0: None, 1: None}
        lnb_out = {0: None, 1: None}
        pend = None
        for t in range(T + 1):
            if t < T:
                half = t % 2
                lslice = lnb[:, half * D:(half + 1) * D]
                aff = p.emit("scalar", lambda e, tt=t, ls=lslice: e.activation(
                    out=ls, in_=x_res[:, tt * D:(tt + 1) * D], func=AF.Identity,
                    scale=st_rstd[:, tt:tt + 1], bias=st_nbias[:, tt:tt + 1]),
                    deps=[t_nb, lnb_out[half], lnb_tr[half]], inc="AC",
                    chain=False)
                st = quant_a(lslice, lslice, D, deps=[aff])
            if pend is not None:
                pst, pt = pend
                tok_out = quant_b(pst)
                phalf = pt % 2
                lnb_out[phalf] = tok_out
                pls = lnb[:, phalf * D:(phalf + 1) * D]
                par = pt % 2
                tok = transpose_batch(
                    [pls[:, fc * 128:(fc + 1) * 128] for fc in range(FC)],
                    par, deps=[tok_out])
                lnb_tr[phalf] = tok
                last = tr_evac("vector", par, lambda e, tt=pt, pr=par: e.tensor_copy(
                    out=xq_fm[:, tt * D:(tt + 1) * D], in_=ps_tr[pr][:, 0:D]),
                    deps=[tok, dst_free])
                xq_t[pt] = last
            if t < T:
                pend = (st, t)
        return x_toks, xq_t

    def ffn_phase(x_res, xq_fm, h_buf, wgt_g, wut_g, wdt_g, hq_im,
                  xq_ready, wsrc_ready, out_dram=None, wq_direct=None,
                  IGF=512):
        NGF = cdiv(I, IGF)
        wload = {}
        wread_done = {}
        hq_done = {}
        hT_done = {}
        silu_mul_done = {}
        yacc_done = {}
        dmm_done = {}
        out_toks = [None] * T
        wq_src_free = [[]]
        st_grp = {}
        ev_g = {}
        ev_u = {}
        wd_evac = {}
        hqc_done = {}
        tr_task = {}
        pend_wq = {}

        def wq_chains(gi):
            wraw, wtm, stats_gate = wq_direct
            i0 = gi * IGF
            isz = min(IGF, I - i0)
            if isz <= 0:
                return
            nrc = cdiv(isz, 128)
            pend_wq[gi] = []
            gate = [stats_gate] if gi == 0 else []
            for wi, which in enumerate(("g", "u", "d")):
                k = 3 * gi + wi
                wt = wtm[k % 3]
                ldt = []
                if which in ("g", "u"):
                    src_h = wg_h if which == "g" else wu_h
                    for rc in range(nrc):
                        ldt.append(dma(wraw[:, rc * D:(rc + 1) * D],
                                  src_h[i0 + rc * 128: i0 + rc * 128 + 128, :],
                                  deps=wq_src_free[0]))
                    width = nrc * D
                else:
                    for rc in range(FC):
                        ldt.append(dma(wraw[:, rc * isz:(rc + 1) * isz],
                                  wd_h[rc * 128:(rc + 1) * 128, i0:i0 + isz],
                                  deps=wq_src_free[0]))
                    width = FC * isz
                sts = []
                for off in range(0, width, QW):
                    w = min(QW, width - off)
                    sts.append(quant_a(wraw[:, off:off + w],
                                       wt[:, off:off + w], w,
                                       deps=[ldt] + gate, src_f32=True,
                                       act_levels=7,
                                       out_war=tr_task.get(k - 3),
                                       sh_scale=(INV12 * 0.5) if which == "d"
                                       else INV12))
                t_q = None
                for st in sts:
                    t_q = quant_b(st)
                wq_src_free[0] = [t for st in sts
                                  for t in (st["t_ax"], st["t_sg"])]
                pend_wq[gi].append((which, t_q, wt, isz, nrc))

        def wq_tails(gi):
            if gi not in pend_wq:
                return
            par = gi % 2
            i0 = gi * IGF
            war_gu = [wread_done.get(gi - 2)] + st_grp.get(gi - 2, [])
            war_d = [dmm_done.get((gi - 2, T - 1))] + st_grp.get(gi - 2, [])
            for wi, (which, t_q, wt, isz, nrc) in enumerate(pend_wq.pop(gi)):
                k = 3 * gi + wi
                tok = None
                if which in ("g", "u"):
                    gbuf = wgt_g[par] if which == "g" else wut_g[par]
                    dst_d = wgT_d if which == "g" else wuT_d
                    for fc in range(FC):
                        tp = fc % 2
                        tok = transpose_batch(
                            [wt[:, rc * D + fc * 128: rc * D + fc * 128 + 128]
                             for rc in range(nrc)], tp, deps=[t_q])
                        ev = tr_evac_act(
                            tp, gbuf[:, fc * IGF: fc * IGF + nrc * 128],
                            nrc * 128, deps=[tok] + war_gu)
                        (ev_g if which == "g" else ev_u).setdefault(
                            gi, {})[fc] = ev
                        stt = dma(dst_d[fc * 128:(fc + 1) * 128, i0:i0 + isz],
                                  gbuf[:, fc * IGF: fc * IGF + isz],
                                  deps=[ev], sem="St")
                        st_grp.setdefault(gi, []).append(stt)
                else:
                    dbuf = wdt_g[par]
                    for ib in range(nrc):
                        tp = ib % 2
                        tok = transpose_batch(
                            [wt[:, hc * isz + ib * 128: hc * isz + ib * 128 + 128]
                             for hc in range(FC)], tp, deps=[t_q])
                        ev = tr_evac_act(
                            tp, dbuf[:, ib * D:(ib + 1) * D], D,
                            deps=[tok] + war_d)
                        wd_evac.setdefault(gi, []).append(ev)
                        stt = dma(wdT_d[i0 + ib * 128: i0 + ib * 128 + 128, :],
                                  dbuf[:, ib * D:(ib + 1) * D], deps=[ev],
                                  sem="St")
                        st_grp.setdefault(gi, []).append(stt)
                tr_task[k] = tok

        def load_group(g):
            gi0 = g * IGF
            gsz = min(IGF, I - gi0)
            par = g % 2
            dep_gu = [wsrc_ready[g], wread_done.get(g - 2)]
            dep_d = [wsrc_ready[g], dmm_done.get((g - 2, T - 1))]
            for fc in range(FC):
                ev_g.setdefault(g, {})[fc] = dma(
                    wgt_g[par][:, fc * IGF: fc * IGF + gsz],
                    wgT_d[fc * 128:(fc + 1) * 128, gi0:gi0 + gsz], deps=dep_gu)
                ev_u.setdefault(g, {})[fc] = dma(
                    wut_g[par][:, fc * IGF: fc * IGF + gsz],
                    wuT_d[fc * 128:(fc + 1) * 128, gi0:gi0 + gsz], deps=dep_gu)
            for ib in range(cdiv(gsz, 128)):
                wd_evac.setdefault(g, []).append(dma(
                    wdt_g[par][:, ib * D:(ib + 1) * D],
                    wdT_d[gi0 + ib * 128: gi0 + ib * 128 + 128, :], deps=dep_d))

        def gate_up(g):
            gi0 = g * IGF
            gsz = min(IGF, I - gi0)
            par = g % 2
            assert 2 * gsz <= 512 and gsz == IGF
            for pi in range(T // 2):
                t0, t1 = 2 * pi, 2 * pi + 1
                ig, iu = 2 * (pi % 2), 2 * (pi % 2) + 1
                pg, pu = ps_mm[ig], ps_mm[iu]
                tok = None
                for ti, t in ((0, t0), (1, t1)):
                    xr = xq_ready[t] if isinstance(xq_ready, list) else xq_ready
                    for fc in range(FC):
                        d = [ev_g[g][fc]]
                        if fc == 0:
                            d += [xr]
                            if ti == 0:
                                d += [mm_free[ig], mm_free[iu]]
                        tok = p.emit("tensor", lambda e, f=fc, ppg=pg, pp=par,
                                     gz=gsz, tt=t, o=ti * gsz: e.matmul(
                            ppg[:, o:o + gz],
                            xq_fm[:, tt * D + f * 128: tt * D + f * 128 + 128],
                            wgt_g[pp][:, f * IGF: f * IGF + gz],
                            start=(f == 0), stop=(f == FC - 1)),
                            deps=d, inc="PE")
                for ti, t in ((0, t0), (1, t1)):
                    for fc in range(FC):
                        tok = p.emit("tensor", lambda e, f=fc, ppu=pu, pp=par,
                                     gz=gsz, tt=t, o=ti * gsz: e.matmul(
                            ppu[:, o:o + gz],
                            xq_fm[:, tt * D + f * 128: tt * D + f * 128 + 128],
                            wut_g[pp][:, f * IGF: f * IGF + gz],
                            start=(f == 0), stop=(f == FC - 1)),
                            deps=[ev_u[g][fc]], inc="PE")
                if t1 == T - 1:
                    wread_done[g] = tok
                t_sil = p.emit("scalar", lambda e, tt=t0, ppg=pg, gz=gsz,
                               hb=h_buf[g % 2]: e.activation(
                    out=hb[:, tt * IGF: tt * IGF + 2 * gz],
                    in_=ppg[:, 0:2 * gz], func=AF.Silu),
                    deps=[tok, hT_done.get((g - 2, t0)),
                          hT_done.get((g - 2, t1)), hq_done.get(g - 2)],
                    inc="AC", chain=False)
                t_mul = p.emit("vector", lambda e, tt=t0, ppu=pu, gz=gsz,
                               hb=h_buf[g % 2]: e.tensor_tensor(
                    out=hb[:, tt * IGF: tt * IGF + 2 * gz],
                    in0=hb[:, tt * IGF: tt * IGF + 2 * gz],
                    in1=ppu[:, 0:2 * gz], op=ALU.mult),
                    deps=[t_sil], inc="DV", chain=False)
                silu_mul_done[(g, t0)] = t_mul
                silu_mul_done[(g, t1)] = t_mul
                mm_free[ig] = t_sil
                mm_free[iu] = t_mul

        def quant_h(g):
            t_q = None
            pend = None
            for ci, off in enumerate(range(0, T * IGF, QW)):
                lt = min(T - 1, (off + QW - 1) // IGF)
                st = quant_a(h_buf[g % 2][:, off:off + QW],
                             h_buf[g % 2][:, off:off + QW], QW,
                             deps=[silu_mul_done[(g, lt)]])
                if pend is not None:
                    t_q = quant_b(pend[0])
                    hqc_done[(g, pend[1])] = t_q
                pend = (st, ci)
            t_q = quant_b(pend[0])
            hqc_done[(g, pend[1])] = t_q
            hq_done[g] = t_q

        def down(g):
            gi0 = g * IGF
            gsz = min(IGF, I - gi0)
            nib = cdiv(gsz, 128)
            par = g % 2
            for t in range(T):
                trp = t % 2
                tok = transpose_batch(
                    [h_buf[g % 2][:, t * IGF + ib * 128: t * IGF + ib * 128 + 128]
                     for ib in range(nib)], trp,
                    deps=[hqc_done[(g, (t * IGF) // QW)]])
                prev_dm = dmm_done.get((g, t - 2)) or dmm_done.get((g - 1, T - 2 + (t % 2)))
                ev = tr_evac_act(trp, hq_im[trp][:, 0:nib * 128], nib * 128,
                                 deps=[tok, prev_dm])
                hT_done[(g, t)] = ev
                dm_toks = [None, None]
                for hh in range(2):
                    psy = ps_y[hh]
                    p.emit("tensor", lambda e, h_=hh, py=psy, tt=t: e.matmul(
                        py[:], ident_f[:],
                        x_res[:, tt * D + h_ * 512: tt * D + h_ * 512 + 512],
                        start=True, stop=False),
                        deps=(([ev] + wd_evac[g] if hh == 0 else [])
                              + [y_free[hh]]), inc="PE")
                    for ib in range(nib):
                        dm_toks[hh] = p.emit("tensor", lambda e, b_=ib, h_=hh,
                                             py=psy, pr=trp, pp=par,
                                             hi=hq_im[trp]: e.matmul(
                            py[:], hi[:, b_ * 128:(b_ + 1) * 128],
                            wdt_g[pp][:, b_ * D + h_ * 512: b_ * D + h_ * 512 + 512],
                            start=False, stop=(b_ == nib - 1)),
                            deps=(), inc="PE")
                tok2 = dm_toks[1]
                dmm_done[(g, t)] = tok2
                ya = None
                for hh in range(2):
                    psy = ps_y[hh]
                    ya = p.emit("scalar", lambda e, tt=t, h_=hh, py=psy: e.activation(
                        out=x_res[:, tt * D + h_ * 512: tt * D + h_ * 512 + 512],
                        in_=py[:], func=AF.Identity),
                        deps=[dm_toks[hh]], inc="AC", chain=False)
                    y_free[hh] = ya
                yacc_done[(g, t)] = ya
                if g == NGF - 1 and out_dram is not None:
                    out_toks[t] = dma(out_dram[t * 128:(t + 1) * 128, :],
                                      x_res[:, t * D:(t + 1) * D],
                                      deps=[ya], sem="St")

        FENCE_FFN = bool(globals().get("_FFN_FENCE", False))
        if wq_direct:
            wq_chains(0)
            wq_tails(0)
            if NGF > 1:
                wq_chains(1)
        else:
            load_group(0)
            if NGF > 1:
                load_group(1)
        for g in range(NGF):
            if FENCE_FFN:
                p.fence()
            gate_up(g)
            if FENCE_FFN:
                p.fence()
            if g >= 1:
                down(g - 1)
            if FENCE_FFN:
                p.fence()
            if wq_direct:
                if g + 1 < NGF:
                    wq_tails(g + 1)
                quant_h(g)
                if g + 2 < NGF:
                    wq_chains(g + 2)
            else:
                quant_h(g)
                if g + 1 < NGF:
                    load_group(g + 1)
        if FENCE_FFN:
            p.fence()
        down(NGF - 1)
        grp_done = [st_grp.get(g, []) for g in range(NGF)]
        return out_toks, yacc_done[(NGF - 1, T - 1)], grp_done

    if DBG and (DBGMASK & 2):
        xres_h = nc.dram_tensor("xres_dbg", [128, T * D], F32, kind="ExternalOutput").ap()
        st_h = nc.dram_tensor("st_dbg", [128, 4 * T], F32, kind="ExternalOutput").ap()
    if DBG and (DBGMASK & 1):
        stpre = sq_es.enter_context(nc.sbuf_tensor(
            "stpre", [128, 2 * T], F32, side="left"))
        stpre_h = nc.dram_tensor("stpre_dbg", [128, 2 * T], F32, kind="ExternalOutput").ap()
    x_toks, xq1_ready = lnq_phase(x_h, x_res, xq_fm, None, None,
                                  lnq_dbg=[stpre] if (DBG and (DBGMASK & 1)) else None)
    import os
    wqd = os.environ.get("WQ_DELAY")
    if wqd:
        if wqd == "full":
            p.emit("sync", lambda e: e.nop(), deps=[xq1_ready], inc=None)
        else:
            sname, val = wqd.split(":")
            p.emit("sync", lambda e: e.nop(), deps=[(sname, int(val))], inc=None)
    if DBG and (DBGMASK & 1):
        dma(stpre_h, stpre[:], deps=[xq1_ready], sem="St")
    if DBG and (DBGMASK & 2):
        dma(xres_h, x_res[:], deps=[x_toks, xq1_ready], sem="St")
        dma(st_h[:, 0:T], st_sum[:], deps=[xq1_ready], sem="St")
        dma(st_h[:, T:2*T], st_sq[:], deps=[xq1_ready], sem="St")
        dma(st_h[:, 2*T:3*T], st_rstd[:], deps=[xq1_ready], sem="St")
        dma(st_h[:, 3*T:4*T], st_nbias[:], deps=[xq1_ready], sem="St")
    if DBG and (DBGMASK & 2):
        xqd_h = nc.dram_tensor("xq_dbg", [128, T * D], BF16, kind="ExternalOutput").ap()
        dma(xqd_h, xq_fm[:], deps=[xq1_ready], sem="St")
    hq_im = [lnb_ref[0][:, 0:512], lnb_ref[0][:, 512:1024]]

    _, ffn1_done, slab_done = ffn_phase(
        x_res, xq_fm, h_buf, wgt_g, wut_g, wdt_g, hq_im,
        xq1_ready, None, out_dram=None,
        wq_direct=(wraw, wtm, stat_prev[0]), IGF=IG1)

    p.fence()
    rs1c.close()

    lnx2_es = contextlib.ExitStack()
    lnx2_fm = lnx2_es.enter_context(
        nc.sbuf_tensor("lnx2_fm", [128, FC * S], BF16, side="left"))
    sq_ln2 = lnx2_es.enter_context(
        nc.sbuf_tensor("sq_ln2", [128, D], F32, side="left"))
    stat_dump[0] = sq_ln2[:]
    spill_toks = []
    lnb = lnb_ref[0]
    st_last = None
    for t in range(T):
        st_last = ln_stats([ffn1_done], x_res[:, t * D:(t + 1) * D], t)
        spill_toks.append(dma(x1_d[t * 128:(t + 1) * 128, :],
                              x_res[:, t * D:(t + 1) * D],
                              deps=[ffn1_done], sem="St"))
    t_nb2 = ln_finalize([st_last])
    ln2_last = None
    ln2_tr = {0: None, 1: None}
    for t in range(T):
        half = t % 2
        lslice = lnb[:, half * D:(half + 1) * D]
        aff = p.emit("scalar", lambda e, tt=t, ls=lslice: e.activation(
            out=ls, in_=x_res[:, tt * D:(tt + 1) * D], func=AF.Identity,
            scale=st_rstd[:, tt:tt + 1], bias=st_nbias[:, tt:tt + 1]),
            deps=[t_nb2, ln2_tr[half]], inc="AC", chain=False)
        par = t % 2
        tok = transpose_batch(
            [lslice[:, fc * 128:(fc + 1) * 128] for fc in range(FC)],
            par, deps=[aff])
        ln2_tr[half] = tok
        ev = None
        for fc in range(FC):
            ev = p.emit("vector", lambda e, f=fc, tt=t, pr=par: e.tensor_copy(
                out=lnx2_fm[:, f * S + tt * 128: f * S + tt * 128 + 128],
                in_=ps_tr[pr][:, f * 128:(f + 1) * 128]),
                deps=[tok] if fc == 0 else (), inc="DV", chain=False)
        tr_free[par] = ev
        ln2_last = ev
    lnx2_ready = ln2_last
    if DBG and (DBGMASK & 8):
        lnx2d_h = nc.dram_tensor("lnx2_dbg", [128, FC * S], BF16, kind="ExternalOutput").ap()
        rstd_h = nc.dram_tensor("rstd_dbg", [128, T], F32, kind="ExternalOutput").ap()
        dma(lnx2d_h, lnx2_fm[:], deps=[lnx2_ready], sem="St")
        dma(rstd_h, st_rstd[:], deps=[lnx2_ready], sem="St")

    p.fence()
    rs1b.close()
    rs1.close()
    rsA = contextlib.ExitStack()
    k_fm = sbR(rsA, "k_fm", [128, FC * S], BF16)
    v_tm = sbR(rsA, "v_tm", [128, T * D], BF16)
    q_fm = sbR(rsA, "q_fm", [128, FC * S], BF16)
    rsAw = contextlib.ExitStack()
    wq_raw = sbR(rsAw, "wq_raw", [128, FC * 512], F32)
    wq_bf = sbR(rsAw, "wq_bf", [128, FC * 512], BF16)

    qkv_ev = None
    wq_read = {}
    wcast_done = {}
    for nck in range(6):
        t = []
        for fc in range(FC):
            t.append(dma(wq_raw[:, fc * 512:(fc + 1) * 512],
                    wqkv_h[fc * 128:(fc + 1) * 128, nck * 512:(nck + 1) * 512],
                    deps=[wcast_done.get(nck - 1)]))
        t_cast = p.emit("vector", lambda e: e.tensor_copy(
            out=wq_bf[:], in_=wq_raw[:]),
            deps=[t, wq_read.get(nck - 1)], inc="DV", chain=False)
        wcast_done[nck] = t_cast
        if nck < 4:
            dst = q_fm if nck < 2 else k_fm
            bias_off = 0 if nck < 2 else FC
            for nsub in range(4):
                dq = (nck % 2) * 4 + nsub
                for tc in range(4):
                    mi = (nsub * 4 + tc) % 4
                    pp = ps_mm[mi]
                    tok = None
                    for fc in range(FC):
                        tok = p.emit("tensor", lambda e, f=fc, ns=nsub, tcc=tc, ppp=pp: e.matmul(
                            ppp[:],
                            wq_bf[:, f * 512 + ns * 128: f * 512 + ns * 128 + 128],
                            lnx2_fm[:, f * S + tcc * 512: f * S + (tcc + 1) * 512],
                            start=(f == 0), stop=(f == FC - 1)),
                            deps=[t_cast, lnx2_ready, mm_free[mi]] if fc == 0 else (),
                            inc="PE")
                    wq_read[nck] = tok
                    qkv_ev = p.emit("scalar", lambda e, d_=dst, dqq=dq, tcc=tc,
                                    bo=bias_off, ppp=pp: e.activation(
                        out=d_[:, dqq * S + tcc * 512: dqq * S + (tcc + 1) * 512],
                        in_=ppp[:], func=AF.Identity,
                        bias=bqkv_t[:, bo + dqq: bo + dqq + 1]),
                        deps=[tok, bias_ready], inc="AC", chain=False)
                    mm_free[mi] = qkv_ev
        else:
            dvoff = (nck - 4) * 512
            for t_ in range(T):
                mi = t_ % 4
                pp = ps_mm[mi]
                tok = None
                for fc in range(FC):
                    tok = p.emit("tensor", lambda e, f=fc, tt=t_, ppp=pp: e.matmul(
                        ppp[:],
                        lnx2_fm[:, f * S + tt * 128: f * S + tt * 128 + 128],
                        wq_bf[:, f * 512:(f + 1) * 512],
                        start=(f == 0), stop=(f == FC - 1)),
                        deps=[t_cast, lnx2_ready, mm_free[mi]] if fc == 0 else (),
                        inc="PE")
                wq_read[nck] = tok
                qkv_ev = p.emit("scalar", lambda e, tt=t_, dv=dvoff, ppp=pp: e.activation(
                    out=v_tm[:, tt * D + dv: tt * D + dv + 512], in_=ppp[:],
                    func=AF.Identity), deps=[tok], inc="AC", chain=False)
                mm_free[mi] = qkv_ev
    qkv_done = qkv_ev
    if DBG and (DBGMASK & 4):
        qd_h = nc.dram_tensor("q_dbg", [128, FC * S], BF16, kind="ExternalOutput").ap()
        kd_h = nc.dram_tensor("k_dbg", [128, FC * S], BF16, kind="ExternalOutput").ap()
        vd_h = nc.dram_tensor("v_dbg", [128, T * D], BF16, kind="ExternalOutput").ap()
        dma(qd_h, q_fm[:], deps=[qkv_done], sem="St")
        dma(kd_h, k_fm[:], deps=[qkv_done], sem="St")
        dma(vd_h, v_tm[:], deps=[qkv_done], sem="St")

    p.fence()
    rsAw.close()
    lnx2_es.close()
    sq_es.close()
    rsB = contextlib.ExitStack()
    wo_raw = sbR(rsB, "wo_raw", [128, FC * 256], F32)
    wo_bf = sbR(rsB, "wo_bf", [128, FC * D], BF16)
    s_sb = sbR(rsB, "s_sb", [128, S], F32)
    p_sb = sbR(rsB, "p_sb", [128, S], BF16)
    pT_sb = sbR(rsB, "pT_sb", [128, T * 512], BF16)
    ctx_sb = sbR(rsB, "ctx_sb", [128, FC * 512], BF16)
    o_sb = sbR(rsB, "o_sb", [128, FC * 512], BF16)
    x1_t = sbR(rsB, "x1_t", [128, D], F32)
    sm_m = sbR(rsB, "sm_m", [128, 8], F32)

    t_wo = None
    for rnd in range(4):
        t = []
        for fc in range(FC):
            t.append(dma(wo_raw[:, fc * 256:(fc + 1) * 256],
                    wout_h[fc * 128:(fc + 1) * 128, rnd * 256:(rnd + 1) * 256],
                    deps=[t_wo]))
        half = rnd // 2
        off = (rnd % 2) * 256
        t_wo = p.emit("vector", lambda e, h_=half, o_=off: e.tensor_copy(
            out=wo_bf[:].rearrange("p (h f w) -> p h f w", h=2, f=FC)[
                :, h_, :, o_:o_ + 256],
            in_=wo_raw[:].rearrange("p (f w) -> p f w", w=256)),
            deps=[t], inc="DV", chain=False)

    x2_toks = [None] * T
    x1_load = {}
    if DBG and (DBGMASK & 16):
        pd_h = nc.dram_tensor("p_dbg", [128, S], BF16, kind="ExternalOutput").ap()
        sd_h = nc.dram_tensor("s_dbg", [128, S], F32, kind="ExternalOutput").ap()
        smd_h = nc.dram_tensor("sm_dbg", [128, 8], F32, kind="ExternalOutput").ap()
        ctxd_h = nc.dram_tensor("ctx_dbg", [128, FC * 512], BF16, kind="ExternalOutput").ap()
        od_h = nc.dram_tensor("o_dbg", [128, FC * 512], BF16, kind="ExternalOutput").ap()
    for qg in range(4):
        pT_evs = []
        tr_toks = {}
        for k in range(4):
            qt = qg * 4 + k
            ev_toks = []
            for kc in range(4):
                pp = ps_mm[kc]
                tok = None
                for fc in range(FC):
                    tok = p.emit("tensor", lambda e, f=fc, q_=qt, kcc=kc, ppp=pp: e.matmul(
                        ppp[:],
                        q_fm[:, f * S + q_ * 128: f * S + q_ * 128 + 128],
                        k_fm[:, f * S + kcc * 512: f * S + (kcc + 1) * 512],
                        start=(f == 0), stop=(f == FC - 1)),
                        deps=[qkv_done, mm_free[kc]] if fc == 0 else (), inc="PE")
                ev = p.emit("vector", lambda e, kcc=kc, ppp=pp: e.tensor_copy(
                    out=s_sb[:, kcc * 512:(kcc + 1) * 512], in_=ppp[:]),
                    deps=[tok, (("AC", p.cnt["AC"]) if (qg or k) else None)],
                    inc="DV", chain=False)
                ev_toks.append(ev)
                mm_free[kc] = ev
            t_m = p.emit("vector", lambda e: e.tensor_reduce(
                out=sm_m[:, 0:1], in_=s_sb[:], axis=AX.X, op=ALU.max, negate=True),
                deps=ev_toks, inc="DV")
            t_m = p.emit("vector", lambda e: e.tensor_scalar(
                out=sm_m[:, 0:1], in0=sm_m[:, 0:1], scalar1=1.0 / 32.0,
                scalar2=None, op0=ALU.mult), inc="DV")
            t_e = p.emit("scalar", lambda e, kk=k: e.activation(
                out=p_sb[:], in_=s_sb[:], func=AF.Exp, scale=1.0 / 32.0,
                bias=sm_m[:, 0:1], accum_out=sm_m[:, 1 + kk:2 + kk]),
                deps=[t_m, tr_toks.get(k - 1)], inc="AC", chain=False)
            t_r = p.emit("scalar", lambda e, kk=k: e.activation(
                out=sm_m[:, 5:6], in_=sm_m[:, 1 + kk:2 + kk], func=AF.Ln), inc="AC")
            t_r = p.emit("scalar", lambda e: e.activation(
                out=sm_m[:, 5:6], in_=sm_m[:, 5:6], func=AF.Exp, scale=-1.0),
                inc="AC")
            t_pn = p.emit("vector", lambda e: e.tensor_scalar(
                out=p_sb[:], in0=p_sb[:], scalar1=sm_m[:, 5:6], scalar2=None,
                op0=ALU.mult), deps=[t_r], inc="DV", chain=False)
            if DBG and (DBGMASK & 16) and qg == 0 and k == 0:
                dma(pd_h, p_sb[:], deps=[t_pn], sem="St")
                dma(sd_h, s_sb[:], deps=[t_pn], sem="St")
                dma(smd_h, sm_m[:], deps=[t_pn], sem="St")
            for half in range(2):
                par = half
                tok = transpose_batch(
                    [p_sb[:, (half * 8 + kt) * 128:(half * 8 + kt) * 128 + 128]
                     for kt in range(8)], par, deps=[t_pn])
                tr_toks[k] = tok
                ev = tr_evac("vector", par, lambda e, h_=half, kk=k, pr=par: e.tensor_copy(
                    out=pT_sb[:].rearrange("p (kt w) -> p kt w", w=512)[
                        :, h_ * 8:(h_ + 1) * 8, kk * 128:(kk + 1) * 128],
                    in_=ps_tr[pr][:].rearrange("p (b w) -> p b w", w=128)),
                    deps=[tok])
                pT_evs.append(ev)
        av_ev = None
        for dc in range(FC):
            mi = dc % 4
            pp = ps_mm[mi]
            tok = None
            for kt in range(T):
                tok = p.emit("tensor", lambda e, ktt=kt, d_=dc, ppp=pp: e.matmul(
                    ppp[:],
                    v_tm[:, ktt * D + d_ * 128: ktt * D + d_ * 128 + 128],
                    pT_sb[:, ktt * 512:(ktt + 1) * 512],
                    start=(ktt == 0), stop=(ktt == T - 1)),
                    deps=(pT_evs + [mm_free[mi]]) if kt == 0 else (), inc="PE")
            av_ev = p.emit("scalar", lambda e, d_=dc, ppp=pp: e.activation(
                out=ctx_sb[:, d_ * 512:(d_ + 1) * 512], in_=ppp[:],
                func=AF.Identity,
                bias=bqkv_t[:, 2 * FC + d_: 2 * FC + d_ + 1]),
                deps=[tok], inc="AC", chain=False)
            mm_free[mi] = av_ev
        oj_ev = None
        for oc in range(FC):
            mi = oc % 4
            pp = ps_mm[mi]
            woff = (oc // 4) * FC * 512 + (oc % 4) * 128
            tok = None
            for dc in range(FC):
                tok = p.emit("tensor", lambda e, d_=dc, ppp=pp, wo=woff: e.matmul(
                    ppp[:], wo_bf[:, d_ * 512 + wo: d_ * 512 + wo + 128],
                    ctx_sb[:, d_ * 512:(d_ + 1) * 512],
                    start=(d_ == 0), stop=(d_ == FC - 1)),
                    deps=[av_ev, t_wo, mm_free[mi]] if dc == 0 else (), inc="PE")
            oj_ev = p.emit("scalar", lambda e, o_=oc, ppp=pp: e.activation(
                out=o_sb[:, o_ * 512:(o_ + 1) * 512], in_=ppp[:],
                func=AF.Identity, bias=bout_t[:, o_: o_ + 1]),
                deps=[tok], inc="AC", chain=False)
            mm_free[mi] = oj_ev
        if DBG and (DBGMASK & 16) and qg == 0:
            dma(ctxd_h, ctx_sb[:], deps=[av_ev], sem="St")
            dma(od_h, o_sb[:], deps=[oj_ev], sem="St")
        for k in range(4):
            t_abs = qg * 4 + k
            x1_load[t_abs] = dma(
                x1_t[:],
                x1_d[t_abs * 128:(t_abs + 1) * 128, :],
                deps=[spill_toks[t_abs], x2_toks[t_abs - 1] if t_abs >= 1 else None])
            par = k % 2
            tok = transpose_batch(
                [o_sb[:, oc * 512 + k * 128: oc * 512 + k * 128 + 128]
                 for oc in range(FC)], par, deps=[oj_ev])
            t_add = tr_evac("vector", par, lambda e, pr=par: e.tensor_tensor(
                out=x1_t[:], in0=x1_t[:], in1=ps_tr[pr][:], op=ALU.add),
                deps=[tok, x1_load[t_abs]])
            x2_toks[t_abs] = dma(x2_d[t_abs * 128:(t_abs + 1) * 128, :],
                                 x1_t[:], deps=[t_add], sem="St")

    p.fence()
    rsB.close()
    rsA.close()
    rs2 = contextlib.ExitStack()
    x_res2 = sbR(rs2, "x_res2", [128, T * D], F32)
    lnb_ref[0] = sbR(rs2, "lnb2", [128, 2 * D], BF16)
    xq_fm2 = sbR(rs2, "xq_fm2", [128, T * D], BF16)
    h_buf2 = [sbR(rs2, "h2_0", [128, T * IG1], BF16),
              sbR(rs2, "h2_1", [128, T * IG1], BF16)]
    wgt_g2 = [sbR(rs2, "wgt2_0", [128, FC * IG1], BF16),
              sbR(rs2, "wgt2_1", [128, FC * IG1], BF16)]
    wut_g2 = [sbR(rs2, "wut2_0", [128, FC * IG1], BF16),
              sbR(rs2, "wut2_1", [128, FC * IG1], BF16)]
    wdt_g2 = [sbR(rs2, "wdt2_0", [128, 2 * D], BF16),
              sbR(rs2, "wdt2_1", [128, 2 * D], BF16)]
    hq_im2 = [lnb_ref[0][:, 0:512], lnb_ref[0][:, 512:1024]]
    sq_sc2 = sbR(rs2, "sq_sc2", [128, D], F32)
    stat_dump[0] = sq_sc2[:]

    _, xq2_ready = lnq_phase(x2_d, x_res2, xq_fm2, x2_toks, None)
    out_toks, _, _ = ffn_phase(x_res2, xq_fm2, h_buf2, wgt_g2, wut_g2, wdt_g2,
                               hq_im2, xq2_ready, slab_done, out_dram=y_h,
                               IGF=IG1)

    p.emit("sync", lambda e: e.nop(), deps=[st_fence_toks()], inc=None)

    p.check()
    with nc.Block() as block:
        p.realize(block)
    rs2.close()
    es.close()
    return nc


_NC_CACHE = {}


def kernel(L0_attn_x, L0_attn_W_qkv, L0_attn_b_qkv, L0_attn_W_out, L0_attn_b_out,
           L0_ffn_w_gate, L0_ffn_w_up, L0_ffn_w_down):
    B, S, D = L0_attn_x.shape
    key = (S, D)
    if key not in _NC_CACHE:
        _NC_CACHE[key] = build_nc(S=S, D=D, I=L0_ffn_w_gate.shape[0])
    nc = _NC_CACHE[key]
    wqkv = np.ascontiguousarray(L0_attn_W_qkv, dtype=np.float32)
    bqkv = np.ascontiguousarray(L0_attn_b_qkv, dtype=np.float32)
    wout = np.ascontiguousarray(L0_attn_W_out, dtype=np.float32)
    bout = np.ascontiguousarray(L0_attn_b_out, dtype=np.float32)
    wg = np.ascontiguousarray(L0_ffn_w_gate, dtype=np.float32)
    wu = np.ascontiguousarray(L0_ffn_w_up, dtype=np.float32)
    wd = np.ascontiguousarray(L0_ffn_w_down, dtype=np.float32)
    in_maps = []
    for i in range(B):
        in_maps.append({
            "x": np.ascontiguousarray(L0_attn_x[i], dtype=np.float32),
            "wqkv": wqkv, "bqkv": bqkv, "wout": wout, "bout": bout,
            "wg": wg, "wu": wu, "wd": wd,
        })
    res = run_bass_kernel_spmd(nc, in_maps, core_ids=list(range(B)))
    out = np.stack([np.asarray(r["y"]) for r in res.results], axis=0)
    return out.astype(np.float32)


# version 25
# speedup vs baseline: 1.1357x; 1.1357x over previous
import contextlib
import numpy as np

import concourse.bass as bass
import concourse.mybir as mybir
from concourse.bass_utils import run_bass_kernel_spmd

F32 = mybir.dt.float32
BF16 = mybir.dt.bfloat16
ALU = mybir.AluOpType
AF = mybir.ActivationFunctionType
AX = mybir.AxisListType

ENGS = ("sync", "scalar", "vector", "gpsimd", "tensor")
LN6 = float(np.log(np.float32(6.0)))
INV12 = float(np.float32(1.0 / 12.0))


class Prog:

    def __init__(self, nc):
        self.nc = nc
        self.q = {e: [] for e in ENGS}
        self.cnt = {}
        self.seen = {e: {} for e in ENGS}
        self.sems = {}
        self.fence_toks = []
        self.last_chained = {}

    def sem(self, es, name):
        self.sems[name] = es.enter_context(self.nc.semaphore(name))
        self.cnt[name] = 0

    def tok(self, sem):
        return (sem, self.cnt[sem])

    def fence(self):
        self.fence_toks = [(s, c) for s, c in self.cnt.items() if c > 0]

    OWN = {"vector": "DV", "scalar": "AC", "gpsimd": "PO"}

    def emit(self, eng, fn, deps=(), inc=None, inc_by=1, chain=True):
        w = []
        flat = []
        own = self.OWN.get(eng)
        if own is not None and self.cnt[own] > 0 and (chain or self.last_chained.get(eng, True)):
            flat.append((own, self.cnt[own]))
        self.last_chained[eng] = chain
        for d in list(deps) + self.fence_toks:
            if d is None:
                continue
            if isinstance(d, list):
                flat.extend(x for x in d if x is not None)
            else:
                flat.append(d)
        for d in flat:
            s, v = d
            if v <= 0 or self.seen[eng].get(s, 0) >= v:
                continue
            self.seen[eng][s] = v
            w.append((s, v))
        tok = None
        rec_inc = None
        if inc is not None:
            self.cnt[inc] += inc_by
            tok = (inc, self.cnt[inc])
            if eng not in self.OWN:
                self.seen[eng][inc] = self.cnt[inc]
            rec_inc = (inc, inc_by)
        self.q[eng].append((fn, w, rec_inc))
        return tok

    def check(self):
        ptr = {e: 0 for e in ENGS}
        cnt = {s: 0 for s in self.cnt}
        progress = True
        while progress:
            progress = False
            for e in ENGS:
                while ptr[e] < len(self.q[e]):
                    fn, waits, inc = self.q[e][ptr[e]]
                    if any(cnt[s] < v for s, v in waits):
                        break
                    if inc is not None:
                        cnt[inc[0]] += inc[1]
                    ptr[e] += 1
                    progress = True
        stuck = {e: (len(self.q[e]) - ptr[e]) for e in ENGS if ptr[e] < len(self.q[e])}
        if stuck:
            detail = {}
            for e in stuck:
                fn, waits, inc = self.q[e][ptr[e]]
                detail[e] = [(s, v, cnt[s]) for s, v in waits if cnt[s] < v]
            raise RuntimeError(f"deadlock: stuck ops {stuck}; heads wait {detail}")

    def realize(self, block):
        sems = self.sems

        def runner(eng_name):
            ops = self.q[eng_name]

            def run(eng):
                for fn, waits, inc in ops:
                    for s, v in waits:
                        eng.wait_ge(sems[s], v)
                    inst = fn(eng)
                    if inc is not None:
                        inst.then_inc(sems[inc[0]], inc[1])
            return run

        block.sync(runner("sync"))
        block.scalar(runner("scalar"))
        block.vector(runner("vector"))
        block.gpsimd(runner("gpsimd"))
        block.tensor(runner("tensor"))


def cdiv(a, b):
    return (a + b - 1) // b


def build_nc(S=2048, D=1024, I=2816, DBG=False, FFN_FENCE=False, LNQ_FENCE=False, EXT_KIND=False, DBGMASK=31):
    globals()["_FFN_FENCE"] = FFN_FENCE
    globals()["_LNQ_FENCE"] = LNQ_FENCE
    T = S // 128
    FC = D // 128
    IG = 512
    NG = cdiv(I, IG)
    QW = 1024

    nc = bass.Bass("TRN2", target_bir_lowering=False, detect_race_conditions=False)

    x_h = nc.dram_tensor("x", [S, D], F32, kind="ExternalInput").ap()
    wqkv_h = nc.dram_tensor("wqkv", [D, 3 * D], F32, kind="ExternalInput").ap()
    bqkv_h = nc.dram_tensor("bqkv", [3 * D], F32, kind="ExternalInput").ap()
    wout_h = nc.dram_tensor("wout", [D, D], F32, kind="ExternalInput").ap()
    bout_h = nc.dram_tensor("bout", [D], F32, kind="ExternalInput").ap()
    wg_h = nc.dram_tensor("wg", [I, D], F32, kind="ExternalInput").ap()
    wu_h = nc.dram_tensor("wu", [I, D], F32, kind="ExternalInput").ap()
    wd_h = nc.dram_tensor("wd", [D, I], F32, kind="ExternalInput").ap()
    y_h = nc.dram_tensor("y", [S, D], F32, kind="ExternalOutput").ap()
    wgT_d = nc.dram_tensor("wgT_d", [D, I], BF16, kind="ExternalOutput" if (DBG or EXT_KIND) else "Internal").ap()
    wuT_d = nc.dram_tensor("wuT_d", [D, I], BF16, kind="ExternalOutput" if (DBG or EXT_KIND) else "Internal").ap()
    wdT_d = nc.dram_tensor("wdT_d", [I, D], BF16, kind="ExternalOutput" if (DBG or EXT_KIND) else "Internal").ap()
    x1_d = nc.dram_tensor("x1_d", [S, D], F32, kind="ExternalOutput" if (DBG or EXT_KIND) else "Internal").ap()
    x2_d = nc.dram_tensor("x2_d", [S, D], F32, kind="ExternalOutput" if (DBG or EXT_KIND) else "Internal").ap()

    es = contextlib.ExitStack()

    def sbL(name, shape, dt):
        return es.enter_context(nc.sbuf_tensor(name, shape, dt, side="left"))

    def pst(name, shape, dt):
        return es.enter_context(nc.psum_tensor(name, shape, dt))

    p = Prog(nc)
    for s in ("PE", "DV", "AC", "PO"):
        p.sem(es, s)
    NLD, NST = 16, 8
    for i in range(NLD):
        p.sem(es, f"L{i}")
    for i in range(NST):
        p.sem(es, f"S{i}")
    rotor = {"L": 0, "S": 0}

    def st_fence_toks():
        return [(f"S{i}", p.cnt[f"S{i}"]) for i in range(NST)]

    def ld_fence_toks():
        return [(f"L{i}", p.cnt[f"L{i}"]) for i in range(NLD)]

    ident_b = sbL("ident_b", [128, 128], BF16)
    ident_f = sbL("ident_f", [128, 128], F32)
    bqkv_t = sbL("bqkv_t", [128, 3 * FC], F32)
    bout_t = sbL("bout_t", [128, FC], F32)
    c_ln6 = sbL("c_ln6", [128, 1], F32)
    c_eps = sbL("c_eps", [128, 1], F32)
    nc.const_aps.register(F32, LN6, c_ln6[:])
    nc.const_aps.register(F32, 1e-5, c_eps[:])
    MIDS = (0.25, 0.75, 1.25, 1.75, 2.5, 3.5, 5.0)
    c_mid = []
    for i, m in enumerate(MIDS):
        t = sbL(f"c_mid{i}", [128, 1], F32)
        nc.const_aps.register(F32, -float(m), t[:])
        c_mid.append(t)
    st_sum = sbL("st_sum", [128, T], F32)
    st_sq = sbL("st_sq", [128, T], F32)
    st_rstd = sbL("st_rstd", [128, T], F32)
    st_nbias = sbL("st_nbias", [128, T], F32)
    lnb_ref = [None]
    stat_dump = [None]
    stat_prev = [None]
    wq_vf = [None, None, None]

    class QSet:
        def __init__(self, i):
            self.i = i
            self.v = sbL(f"qs{i}_v", [128, QW], BF16)
            self.a = sbL(f"qs{i}_a", [128, QW], BF16)
            self.b = sbL(f"qs{i}_b", [128, QW], BF16)
            self.c = sbL(f"qs{i}_c", [128, QW], BF16)
            self.d = sbL(f"qs{i}_d", [128, QW], BF16)
            self.g = sbL(f"qs{i}_g", [128, QW], BF16)
            self.bm = sbL(f"qs{i}_bm", [128, QW // 16], F32)
            self.iv = sbL(f"qs{i}_iv", [128, QW // 16], F32)
            self.sh = sbL(f"qs{i}_sh", [128, QW // 16], F32)
            self.out = None
            self.ivtok = None
            self.ztok = None
    qsets = [QSet(0), QSet(1), QSet(2)]
    qrot = [0]
    wqrot = [0]
    sq_es = contextlib.ExitStack()

    rs1 = contextlib.ExitStack()

    def sbR(stk, name, shape, dt):
        return stk.enter_context(nc.sbuf_tensor(name, shape, dt, side="right"))

    x_res = sbR(rs1, "x_res", [128, T * D], F32)
    rs1b = contextlib.ExitStack()
    xq_fm = sbR(rs1b, "xq_fm", [128, T * D], BF16)
    lnb_ref[0] = sbR(rs1b, "lnb1", [128, 2 * D], BF16)
    rs1c = contextlib.ExitStack()
    wq_vf[0] = sbR(rs1c, "wqvf0", [128, QW], F32)
    wq_vf[1] = sbR(rs1c, "wqvf1", [128, QW], F32)
    wraw = sbR(rs1c, "wraw", [128, 2 * D], F32)
    wtm = [sbR(rs1c, "wtm0", [128, 2 * D], BF16),
           sbR(rs1c, "wtm1", [128, 2 * D], BF16),
           sbR(rs1c, "wtm2", [128, 2 * D], BF16)]
    IG1 = 256
    h_buf = [sbR(rs1c, "h_buf0", [128, T * IG1], BF16),
             sbR(rs1c, "h_buf1", [128, T * IG1], BF16)]
    wgt_g = [sbR(rs1c, "wgt_g0", [128, FC * IG1], BF16),
             sbR(rs1c, "wgt_g1", [128, FC * IG1], BF16)]
    wut_g = [sbR(rs1c, "wut_g0", [128, FC * IG1], BF16),
             sbR(rs1c, "wut_g1", [128, FC * IG1], BF16)]
    wdt_g = [sbR(rs1c, "wdt_g0", [128, 2 * D], BF16),
             sbR(rs1c, "wdt_g1", [128, 2 * D], BF16)]
    stat_dump[0] = wq_vf[0][:, 0:1024]
    braw = wq_vf[0][0:32, 0:128]


    ps_mm = [pst(f"ps_mm{i}", [128, 512], F32) for i in range(4)]
    ps_y = [pst(f"ps_y{i}", [128, 512], F32) for i in range(2)]
    ps_tr = [pst(f"ps_tr{i}", [128, 1024], BF16) for i in range(2)]

    tr_free = {0: None, 1: None}
    mm_free = {i: None for i in range(4)}
    y_free = {0: None, 1: None}

    def dma(out, in_, deps=(), sem="Ld"):
        pool, n = ("L", NLD) if sem == "Ld" else ("S", NST)
        s = f"{pool}{rotor[pool]}"
        rotor[pool] = (rotor[pool] + 1) % n
        prev = (s, p.cnt[s])
        return p.emit("sync", lambda e, o=out, i=in_: e.dma_start(out=o, in_=i),
                      deps=list(deps) + [prev], inc=s, inc_by=16)

    def transpose_batch(blocks, parity, deps=()):
        pt = ps_tr[parity]
        tok = None
        for k, src in enumerate(blocks):
            d = (list(deps) + [tr_free[parity]]) if k == 0 else ()
            tok = p.emit("tensor", lambda e, s=src, kk=k: e.transpose(
                pt[:, kk * 128:(kk + 1) * 128], s, ident_b[:]), deps=d, inc="PE")
        return tok

    def tr_evac(eng, parity, fn, deps=(), inc="DV"):
        tok = p.emit(eng, fn, deps=deps, inc=inc, chain=False)
        tr_free[parity] = tok
        return tok

    def tr_evac_act(parity, out_ap, w, deps=()):
        tok = p.emit("scalar", lambda e, o=out_ap, pr=parity, w_=w:
                     e.activation(out=o, in_=ps_tr[pr][:, 0:w_],
                                  func=AF.Identity),
                     deps=deps, inc="AC", chain=False)
        tr_free[parity] = tok
        return tok

    def quant_a(src_ap, dst_ap, width, deps, src_f32=False, act_levels=3,
                out_war=None, sh_scale=INV12):
        if src_f32:
            S = qsets[wqrot[0] % 2]
            wqrot[0] += 1
        else:
            S = qsets[qrot[0] % len(qsets)]
            qrot[0] += 1
        nb = width // 16
        v = (wq_vf[S.i] if src_f32 else S.v)[:, 0:width]
        bm, iv, sh = S.bm[:, 0:nb], S.iv[:, 0:nb], S.sh[:, 0:nb]
        g = S.g[:, 0:width]
        prev_out, prev_iv = S.out, S.ivtok
        deps = list(deps)
        t_ax = p.emit("scalar", lambda e: e.activation(
            out=v, in_=src_ap, func=AF.Abs), deps=deps + [prev_out],
            inc="AC", chain=False)
        v3 = v.rearrange("p (n b) -> p n b", b=16)
        p.emit("vector", lambda e: e.tensor_reduce(
            out=bm, in_=v3, axis=AX.X, op=ALU.max),
            deps=[t_ax, prev_iv, prev_out], inc="DV")
        t_bc = p.emit("vector", lambda e: e.tensor_scalar(
            out=bm, in0=bm, scalar1=6e-12, scalar2=None, op0=ALU.max), inc="DV")
        t_sh = p.emit("vector", lambda e: e.tensor_scalar(
            out=sh, in0=bm, scalar1=sh_scale, scalar2=None, op0=ALU.mult), inc="DV")
        p.emit("scalar", lambda e: e.activation(
            out=iv, in_=bm, func=AF.Ln), deps=[t_bc], inc="AC")
        t_iv = p.emit("scalar", lambda e: e.activation(
            out=iv, in_=iv, func=AF.Exp, scale=-1.0, bias=LN6), inc="AC")
        t_sg = p.emit("scalar", lambda e: e.activation(
            out=g, in_=src_ap, func=AF.Sign), deps=deps + [prev_out],
            inc="AC", chain=False)
        iv3 = iv.unsqueeze(2).broadcast_to([128, nb, 16])
        t_v = p.emit("gpsimd", lambda e: e.tensor_tensor(
            out=v3, in0=v3, in1=iv3, op=ALU.mult),
            deps=[t_iv, t_ax, prev_out], inc="PO", chain=False)
        g3 = g.rearrange("p (n b) -> p n b", b=16)
        sh3 = sh.unsqueeze(2).broadcast_to([128, nb, 16])
        t_ssg = p.emit("gpsimd", lambda e: e.tensor_tensor(
            out=g3, in0=g3, in1=sh3, op=ALU.mult), deps=[t_sg, t_sh],
            inc="PO", chain=False)
        S.ivtok = t_iv
        return dict(S=S, v=v, g=g, width=width, dst=dst_ap, t_v=t_v,
                    t_ssg=t_ssg, act_levels=act_levels, out_war=out_war,
                    t_ax=t_ax, t_sg=t_sg, ztok_war=S.ztok)

    def quant_b(st):
        S, v, g, width = st["S"], st["v"], st["g"], st["width"]
        A, B, C, Dd = (S.a[:, 0:width], S.b[:, 0:width], S.c[:, 0:width],
                       S.d[:, 0:width])
        t_v = st["t_v"]
        po = S.out
        qtok = st.pop("ztok_war")
        p.emit("vector", lambda e: e.tensor_scalar(
            out=A, in0=v, scalar1=0.25, scalar2=None, op0=ALU.is_gt),
            deps=[t_v], inc="DV", chain=False)
        for mm in (0.75, 1.25):
            p.emit("vector", lambda e, m_=mm: e.scalar_tensor_tensor(
                out=A, in0=v, scalar=m_, in1=A, op0=ALU.is_gt, op1=ALU.add),
                inc="DV", chain=False)
        t3 = p.emit("vector", lambda e: e.scalar_tensor_tensor(
            out=A, in0=v, scalar=1.75, in1=A, op0=ALU.is_gt, op1=ALU.add),
            inc="DV", chain=False)
        s5 = p.emit("scalar", lambda e: e.activation(
            out=B, in_=v, func=AF.Sign, bias=-2.5), deps=[t_v, qtok],
            inc="AC", chain=False)
        s6 = p.emit("scalar", lambda e: e.activation(
            out=C, in_=v, func=AF.Sign, bias=-3.5), deps=[po],
            inc="AC", chain=False)
        s7 = p.emit("scalar", lambda e: e.activation(
            out=Dd, in_=v, func=AF.Sign, bias=-5.0),
            inc="AC", chain=False)
        w1 = p.emit("vector", lambda e: e.tensor_tensor(
            out=B, in0=B, in1=C, op=ALU.add), deps=[s5, s6],
            inc="DV", chain=False)
        w2 = p.emit("vector", lambda e: e.scalar_tensor_tensor(
            out=B, in0=Dd, scalar=2.0, in1=B, op0=ALU.mult, op1=ALU.add),
            deps=[s7, w1], inc="DV", chain=False)
        t_qh = p.emit("vector", lambda e: e.scalar_tensor_tensor(
            out=C, in0=B, scalar=4.0, in1=A, op0=ALU.add, op1=ALU.add),
            deps=[w1, t3], inc="DV", chain=False)
        S.ztok = t_qh
        t_out = p.emit("gpsimd", lambda e, d_=st["dst"], C_=C, g_=g: e.tensor_tensor(
            out=d_, in0=C_, in1=g_, op=ALU.mult),
            deps=[t_qh, st["t_ssg"], st["out_war"]], inc="PO", chain=False)
        S.out = t_out
        return t_out

    def quant(src_ap, dst_ap, width, deps, src_f32=False, act_levels=3):
        return quant_b(quant_a(src_ap, dst_ap, width, deps, src_f32, act_levels))

    def ln_stats(deps, src_tile_ap, col):
        sd = stat_dump[0]
        p.emit("vector", lambda e: e.tensor_reduce(
            out=st_sum[:, col:col + 1], in_=src_tile_ap, axis=AX.X, op=ALU.add),
            deps=deps, inc="DV")
        t_sq = p.emit("scalar", lambda e, sd=sd: e.activation(
            out=sd, in_=src_tile_ap, func=AF.Square),
            deps=list(deps) + [stat_prev[0]], inc="AC", chain=False)
        t_red = p.emit("vector", lambda e, sd=sd: e.tensor_reduce(
            out=st_sq[:, col:col + 1], in_=sd, axis=AX.X, op=ALU.add),
            deps=[t_sq], inc="DV")
        stat_prev[0] = t_red
        return t_red

    def ln_finalize(deps, dbg=None):
        def dcp(slot, src, dep=None):
            if dbg is None:
                return None
            return p.emit("vector", lambda e: e.tensor_copy(
                out=dbg[:, slot * T:(slot + 1) * T], in_=src), deps=[dep] if dep else (), inc="DV")
        p.emit("vector", lambda e: e.tensor_scalar(
            out=st_sum[:], in0=st_sum[:], scalar1=1.0 / D, scalar2=None,
            op0=ALU.mult), deps=deps, inc="DV")
        p.emit("vector", lambda e: e.tensor_scalar(
            out=st_sq[:], in0=st_sq[:], scalar1=1.0 / D, scalar2=None,
            op0=ALU.mult), inc="DV")
        dcp(2, st_sq[:])
        p.emit("vector", lambda e: e.tensor_tensor(
            out=st_rstd[:], in0=st_sum[:], in1=st_sum[:], op=ALU.mult), inc="DV")
        dcp(3, st_rstd[:])
        t_var = p.emit("vector", lambda e: e.tensor_tensor(
            out=st_sq[:], in0=st_sq[:], in1=st_rstd[:], op=ALU.subtract), inc="DV")
        dcp(4, st_sq[:])
        t_ln = p.emit("scalar", lambda e: e.activation(
            out=st_rstd[:], in_=st_sq[:], func=AF.Ln, bias=1e-5),
            deps=[t_var], inc="AC")
        dcp(5, st_rstd[:], t_ln)
        t_r = p.emit("scalar", lambda e: e.activation(
            out=st_rstd[:], in_=st_rstd[:], func=AF.Exp, scale=-0.5),
            deps=[dbg and p.tok("DV")], inc="AC")
        dcp(6, st_rstd[:], t_r)
        return p.emit("vector", lambda e: e.scalar_tensor_tensor(
            out=st_nbias[:], in0=st_sum[:], scalar=-1.0, in1=st_rstd[:],
            op0=ALU.mult, op1=ALU.mult), deps=[t_r], inc="DV")

    p.emit("gpsimd", lambda e: e.memset(ident_b[:], 0.0), inc="PO")
    p.emit("gpsimd", lambda e: e.affine_select(
        out=ident_b[:], in_=ident_b[:], compare_op=ALU.not_equal, fill=1.0,
        base=0, pattern=[[-1, 128]], channel_multiplier=1), inc="PO")
    p.emit("gpsimd", lambda e: e.memset(ident_f[:], 0.0), inc="PO")
    t_idf = p.emit("gpsimd", lambda e: e.affine_select(
        out=ident_f[:], in_=ident_f[:], compare_op=ALU.not_equal, fill=1.0,
        base=0, pattern=[[-1, 128]], channel_multiplier=1), inc="PO")
    p.emit("gpsimd", lambda e: e.memset(c_ln6[:], LN6), inc="PO")
    p.emit("gpsimd", lambda e: e.memset(c_eps[:], 1e-5), inc="PO")
    for i, m in enumerate(MIDS):
        p.emit("gpsimd", lambda e, t=c_mid[i], mm=m: e.memset(t[:], -float(mm)), inc="PO")
    init_tok = p.tok("PO")
    p.fence()

    t_bq = dma(braw[0:3 * FC, :], bqkv_h.rearrange("(c q) -> c q", q=128))
    t_tbq = p.emit("tensor", lambda e: e.transpose(
        ps_y[0][:, 0:3 * FC], braw[0:3 * FC, :], ident_f[0:3 * FC, 0:3 * FC]),
        deps=[t_bq, t_idf], inc="PE")
    t_ebq = p.emit("vector", lambda e: e.tensor_copy(
        out=bqkv_t[:], in_=ps_y[0][:, 0:3 * FC]), deps=[t_tbq], inc="DV", chain=False)
    y_free[0] = t_ebq
    t_bo = dma(braw[0:FC, :], bout_h.rearrange("(c q) -> c q", q=128),
               deps=[t_ebq])
    t_tbo = p.emit("tensor", lambda e: e.transpose(
        ps_y[1][:, 0:FC], braw[0:FC, :], ident_f[0:FC, 0:FC]),
        deps=[t_bo], inc="PE")
    t_ebo = p.emit("vector", lambda e: e.tensor_copy(
        out=bout_t[:], in_=ps_y[1][:, 0:FC]), deps=[t_tbo], inc="DV", chain=False)
    y_free[1] = t_ebo
    bias_ready = t_ebo
    stat_prev[0] = t_ebo

    def lnq_phase(x_src_h, x_res, xq_fm, src_deps, dst_free, lnq_dbg=None):
        lnb = lnb_ref[0]
        """DMA x tiles into x_res, LayerNorm, fp4-quantize, transpose into
        xq_fm (tile-major fc blocks). Returns (x_toks, ready_token)."""
        x_toks = []
        for t in range(T):
            x_toks.append(dma(x_res[:, t * D:(t + 1) * D],
                              x_src_h[t * 128:(t + 1) * 128, :],
                              deps=[src_deps[t] if src_deps else None]))
        xq_t = [None] * T
        st_last = None
        for t in range(T):
            st_last = ln_stats([x_toks[t]], x_res[:, t * D:(t + 1) * D], t)
        if DBG and lnq_dbg:
            p.emit("vector", lambda e: e.tensor_copy(out=lnq_dbg[0][:, 0:T], in_=st_sum[:]),
                   deps=[st_last], inc="DV")
            st_last = p.emit("vector", lambda e: e.tensor_copy(out=lnq_dbg[0][:, T:2*T], in_=st_sq[:]),
                   deps=[st_last], inc="DV")
        t_nb = ln_finalize([st_last])
        if globals().get("_LNQ_FENCE"):
            p.fence()
        last = None
        lnb_tr = {0: None, 1: None}
        lnb_out = {0: None, 1: None}
        pend = None
        for t in range(T + 1):
            if t < T:
                half = t % 2
                lslice = lnb[:, half * D:(half + 1) * D]
                aff = p.emit("scalar", lambda e, tt=t, ls=lslice: e.activation(
                    out=ls, in_=x_res[:, tt * D:(tt + 1) * D], func=AF.Identity,
                    scale=st_rstd[:, tt:tt + 1], bias=st_nbias[:, tt:tt + 1]),
                    deps=[t_nb, lnb_out[half], lnb_tr[half]], inc="AC",
                    chain=False)
                st = quant_a(lslice, lslice, D, deps=[aff])
            if pend is not None:
                pst, pt = pend
                tok_out = quant_b(pst)
                phalf = pt % 2
                lnb_out[phalf] = tok_out
                pls = lnb[:, phalf * D:(phalf + 1) * D]
                par = pt % 2
                tok = transpose_batch(
                    [pls[:, fc * 128:(fc + 1) * 128] for fc in range(FC)],
                    par, deps=[tok_out])
                lnb_tr[phalf] = tok
                last = tr_evac("vector", par, lambda e, tt=pt, pr=par: e.tensor_copy(
                    out=xq_fm[:, tt * D:(tt + 1) * D], in_=ps_tr[pr][:, 0:D]),
                    deps=[tok, dst_free])
                xq_t[pt] = last
            if t < T:
                pend = (st, t)
        return x_toks, xq_t

    def ffn_phase(x_res, xq_fm, h_buf, wgt_g, wut_g, wdt_g, hq_im,
                  xq_ready, wsrc_ready, out_dram=None, wq_direct=None,
                  IGF=512):
        NGF = cdiv(I, IGF)
        wload = {}
        wread_done = {}
        hq_done = {}
        hT_done = {}
        silu_mul_done = {}
        yacc_done = {}
        dmm_done = {}
        out_toks = [None] * T
        wq_src_free = [[]]
        st_grp = {}
        ev_g = {}
        ev_u = {}
        wd_evac = {}
        hqc_done = {}
        tr_task = {}
        pend_wq = {}

        def wq_chains(gi):
            wraw, wtm, stats_gate = wq_direct
            i0 = gi * IGF
            isz = min(IGF, I - i0)
            if isz <= 0:
                return
            nrc = cdiv(isz, 128)
            pend_wq[gi] = []
            gate = [stats_gate] if gi == 0 else []
            for wi, which in enumerate(("g", "u", "d")):
                k = 3 * gi + wi
                wt = wtm[k % 3]
                ldt = []
                if which in ("g", "u"):
                    src_h = wg_h if which == "g" else wu_h
                    for rc in range(nrc):
                        ldt.append(dma(wraw[:, rc * D:(rc + 1) * D],
                                  src_h[i0 + rc * 128: i0 + rc * 128 + 128, :],
                                  deps=wq_src_free[0]))
                    width = nrc * D
                else:
                    for rc in range(FC):
                        ldt.append(dma(wraw[:, rc * isz:(rc + 1) * isz],
                                  wd_h[rc * 128:(rc + 1) * 128, i0:i0 + isz],
                                  deps=wq_src_free[0]))
                    width = FC * isz
                sts = []
                for off in range(0, width, QW):
                    w = min(QW, width - off)
                    sts.append(quant_a(wraw[:, off:off + w],
                                       wt[:, off:off + w], w,
                                       deps=[ldt] + gate, src_f32=True,
                                       act_levels=7,
                                       out_war=tr_task.get(k - 3),
                                       sh_scale=(INV12 * 0.5) if which == "d"
                                       else INV12))
                t_q = None
                for st in sts:
                    t_q = quant_b(st)
                wq_src_free[0] = [t for st in sts
                                  for t in (st["t_ax"], st["t_sg"])]
                pend_wq[gi].append((which, t_q, wt, isz, nrc))

        def wq_tails(gi):
            if gi not in pend_wq:
                return
            par = gi % 2
            i0 = gi * IGF
            war_gu = [wread_done.get(gi - 2)] + st_grp.get(gi - 2, [])
            war_d = [dmm_done.get((gi - 2, T - 1))] + st_grp.get(gi - 2, [])
            for wi, (which, t_q, wt, isz, nrc) in enumerate(pend_wq.pop(gi)):
                k = 3 * gi + wi
                tok = None
                if which in ("g", "u"):
                    gbuf = wgt_g[par] if which == "g" else wut_g[par]
                    dst_d = wgT_d if which == "g" else wuT_d
                    for fc in range(FC):
                        tp = fc % 2
                        tok = transpose_batch(
                            [wt[:, rc * D + fc * 128: rc * D + fc * 128 + 128]
                             for rc in range(nrc)], tp, deps=[t_q])
                        ev = tr_evac_act(
                            tp, gbuf[:, fc * IGF: fc * IGF + nrc * 128],
                            nrc * 128, deps=[tok] + war_gu)
                        (ev_g if which == "g" else ev_u).setdefault(
                            gi, {})[fc] = ev
                        stt = dma(dst_d[fc * 128:(fc + 1) * 128, i0:i0 + isz],
                                  gbuf[:, fc * IGF: fc * IGF + isz],
                                  deps=[ev], sem="St")
                        st_grp.setdefault(gi, []).append(stt)
                else:
                    dbuf = wdt_g[par]
                    for ib in range(nrc):
                        tp = ib % 2
                        tok = transpose_batch(
                            [wt[:, hc * isz + ib * 128: hc * isz + ib * 128 + 128]
                             for hc in range(FC)], tp, deps=[t_q])
                        ev = tr_evac_act(
                            tp, dbuf[:, ib * D:(ib + 1) * D], D,
                            deps=[tok] + war_d)
                        wd_evac.setdefault(gi, []).append(ev)
                        stt = dma(wdT_d[i0 + ib * 128: i0 + ib * 128 + 128, :],
                                  dbuf[:, ib * D:(ib + 1) * D], deps=[ev],
                                  sem="St")
                        st_grp.setdefault(gi, []).append(stt)
                tr_task[k] = tok

        def load_group(g):
            gi0 = g * IGF
            gsz = min(IGF, I - gi0)
            par = g % 2
            dep_gu = [wsrc_ready[g], wread_done.get(g - 2)]
            dep_d = [wsrc_ready[g], dmm_done.get((g - 2, T - 1))]
            for fc in range(FC):
                ev_g.setdefault(g, {})[fc] = dma(
                    wgt_g[par][:, fc * IGF: fc * IGF + gsz],
                    wgT_d[fc * 128:(fc + 1) * 128, gi0:gi0 + gsz], deps=dep_gu)
                ev_u.setdefault(g, {})[fc] = dma(
                    wut_g[par][:, fc * IGF: fc * IGF + gsz],
                    wuT_d[fc * 128:(fc + 1) * 128, gi0:gi0 + gsz], deps=dep_gu)
            for ib in range(cdiv(gsz, 128)):
                wd_evac.setdefault(g, []).append(dma(
                    wdt_g[par][:, ib * D:(ib + 1) * D],
                    wdT_d[gi0 + ib * 128: gi0 + ib * 128 + 128, :], deps=dep_d))

        def gate_up(g):
            gi0 = g * IGF
            gsz = min(IGF, I - gi0)
            par = g % 2
            assert 2 * gsz <= 512 and gsz == IGF
            for pi in range(T // 2):
                t0, t1 = 2 * pi, 2 * pi + 1
                ig, iu = 2 * (pi % 2), 2 * (pi % 2) + 1
                pg, pu = ps_mm[ig], ps_mm[iu]
                tok = None
                for ti, t in ((0, t0), (1, t1)):
                    xr = xq_ready[t] if isinstance(xq_ready, list) else xq_ready
                    for fc in range(FC):
                        d = [ev_g[g][fc]]
                        if fc == 0:
                            d += [xr]
                            if ti == 0:
                                d += [mm_free[ig]]
                        tok = p.emit("tensor", lambda e, f=fc, ppg=pg, pp=par,
                                     gz=gsz, tt=t, o=ti * gsz: e.matmul(
                            ppg[:, o:o + gz],
                            xq_fm[:, tt * D + f * 128: tt * D + f * 128 + 128],
                            wgt_g[pp][:, f * IGF: f * IGF + gz],
                            start=(f == 0), stop=(f == FC - 1)),
                            deps=d, inc="PE")
                for ti, t in ((0, t0), (1, t1)):
                    for fc in range(FC):
                        du = [ev_u[g][fc]]
                        if ti == 0 and fc == 0:
                            du += [mm_free[iu]]
                        tok = p.emit("tensor", lambda e, f=fc, ppu=pu, pp=par,
                                     gz=gsz, tt=t, o=ti * gsz: e.matmul(
                            ppu[:, o:o + gz],
                            xq_fm[:, tt * D + f * 128: tt * D + f * 128 + 128],
                            wut_g[pp][:, f * IGF: f * IGF + gz],
                            start=(f == 0), stop=(f == FC - 1)),
                            deps=du, inc="PE")
                if t1 == T - 1:
                    wread_done[g] = tok
                t_sil = p.emit("scalar", lambda e, tt=t0, ppg=pg, gz=gsz,
                               hb=h_buf[g % 2]: e.activation(
                    out=hb[:, tt * IGF: tt * IGF + 2 * gz],
                    in_=ppg[:, 0:2 * gz], func=AF.Silu),
                    deps=[tok, hT_done.get((g - 2, t0)),
                          hT_done.get((g - 2, t1)), hq_done.get(g - 2)],
                    inc="AC", chain=False)
                t_mul = p.emit("vector", lambda e, tt=t0, ppu=pu, gz=gsz,
                               hb=h_buf[g % 2]: e.tensor_tensor(
                    out=hb[:, tt * IGF: tt * IGF + 2 * gz],
                    in0=hb[:, tt * IGF: tt * IGF + 2 * gz],
                    in1=ppu[:, 0:2 * gz], op=ALU.mult),
                    deps=[t_sil], inc="DV", chain=False)
                silu_mul_done[(g, t0)] = t_mul
                silu_mul_done[(g, t1)] = t_mul
                mm_free[ig] = t_sil
                mm_free[iu] = t_mul

        def quant_h(g):
            t_q = None
            pend = None
            for ci, off in enumerate(range(0, T * IGF, QW)):
                lt = min(T - 1, (off + QW - 1) // IGF)
                st = quant_a(h_buf[g % 2][:, off:off + QW],
                             h_buf[g % 2][:, off:off + QW], QW,
                             deps=[silu_mul_done[(g, lt)]])
                if pend is not None:
                    t_q = quant_b(pend[0])
                    hqc_done[(g, pend[1])] = t_q
                pend = (st, ci)
            t_q = quant_b(pend[0])
            hqc_done[(g, pend[1])] = t_q
            hq_done[g] = t_q

        def down(g):
            gi0 = g * IGF
            gsz = min(IGF, I - gi0)
            nib = cdiv(gsz, 128)
            par = g % 2
            for t in range(T):
                trp = t % 2
                tok = transpose_batch(
                    [h_buf[g % 2][:, t * IGF + ib * 128: t * IGF + ib * 128 + 128]
                     for ib in range(nib)], trp,
                    deps=[hqc_done[(g, (t * IGF) // QW)]])
                prev_dm = dmm_done.get((g, t - 2)) or dmm_done.get((g - 1, T - 2 + (t % 2)))
                ev = tr_evac_act(trp, hq_im[trp][:, 0:nib * 128], nib * 128,
                                 deps=[tok, prev_dm])
                hT_done[(g, t)] = ev
                dm_toks = [None, None]
                for hh in range(2):
                    psy = ps_y[hh]
                    for ib in range(nib):
                        dm_toks[hh] = p.emit("tensor", lambda e, b_=ib, h_=hh,
                                             py=psy, pr=trp, pp=par,
                                             hi=hq_im[trp]: e.matmul(
                            py[:], hi[:, b_ * 128:(b_ + 1) * 128],
                            wdt_g[pp][:, b_ * D + h_ * 512: b_ * D + h_ * 512 + 512],
                            start=(b_ == 0), stop=(b_ == nib - 1)),
                            deps=(([ev] + wd_evac[g] if hh == 0 else [])
                                  + [y_free[hh]]) if ib == 0 else (),
                            inc="PE")
                tok2 = dm_toks[1]
                dmm_done[(g, t)] = tok2
                ya = None
                for hh in range(2):
                    psy = ps_y[hh]
                    ya = p.emit("vector", lambda e, tt=t, h_=hh, py=psy: e.tensor_tensor(
                        out=x_res[:, tt * D + h_ * 512: tt * D + h_ * 512 + 512],
                        in0=x_res[:, tt * D + h_ * 512: tt * D + h_ * 512 + 512],
                        in1=py[:], op=ALU.add),
                        deps=[dm_toks[hh]], inc="DV", chain=False)
                    y_free[hh] = ya
                yacc_done[(g, t)] = ya
                if g == NGF - 1 and out_dram is not None:
                    out_toks[t] = dma(out_dram[t * 128:(t + 1) * 128, :],
                                      x_res[:, t * D:(t + 1) * D],
                                      deps=[ya], sem="St")

        FENCE_FFN = bool(globals().get("_FFN_FENCE", False))
        if wq_direct:
            wq_chains(0)
            wq_tails(0)
            if NGF > 1:
                wq_chains(1)
        else:
            load_group(0)
            if NGF > 1:
                load_group(1)
        for g in range(NGF):
            if FENCE_FFN:
                p.fence()
            gate_up(g)
            if FENCE_FFN:
                p.fence()
            if g >= 1:
                down(g - 1)
            if FENCE_FFN:
                p.fence()
            if wq_direct:
                if g + 1 < NGF:
                    wq_tails(g + 1)
                quant_h(g)
                if g + 2 < NGF:
                    wq_chains(g + 2)
            else:
                quant_h(g)
                if g + 1 < NGF:
                    load_group(g + 1)
        if FENCE_FFN:
            p.fence()
        down(NGF - 1)
        grp_done = [st_grp.get(g, []) for g in range(NGF)]
        return out_toks, yacc_done[(NGF - 1, T - 1)], grp_done

    if DBG and (DBGMASK & 2):
        xres_h = nc.dram_tensor("xres_dbg", [128, T * D], F32, kind="ExternalOutput").ap()
        st_h = nc.dram_tensor("st_dbg", [128, 4 * T], F32, kind="ExternalOutput").ap()
    if DBG and (DBGMASK & 1):
        stpre = sq_es.enter_context(nc.sbuf_tensor(
            "stpre", [128, 2 * T], F32, side="left"))
        stpre_h = nc.dram_tensor("stpre_dbg", [128, 2 * T], F32, kind="ExternalOutput").ap()
    x_toks, xq1_ready = lnq_phase(x_h, x_res, xq_fm, None, None,
                                  lnq_dbg=[stpre] if (DBG and (DBGMASK & 1)) else None)
    import os
    wqd = os.environ.get("WQ_DELAY")
    if wqd:
        if wqd == "full":
            p.emit("sync", lambda e: e.nop(), deps=[xq1_ready], inc=None)
        else:
            sname, val = wqd.split(":")
            p.emit("sync", lambda e: e.nop(), deps=[(sname, int(val))], inc=None)
    if DBG and (DBGMASK & 1):
        dma(stpre_h, stpre[:], deps=[xq1_ready], sem="St")
    if DBG and (DBGMASK & 2):
        dma(xres_h, x_res[:], deps=[x_toks, xq1_ready], sem="St")
        dma(st_h[:, 0:T], st_sum[:], deps=[xq1_ready], sem="St")
        dma(st_h[:, T:2*T], st_sq[:], deps=[xq1_ready], sem="St")
        dma(st_h[:, 2*T:3*T], st_rstd[:], deps=[xq1_ready], sem="St")
        dma(st_h[:, 3*T:4*T], st_nbias[:], deps=[xq1_ready], sem="St")
    if DBG and (DBGMASK & 2):
        xqd_h = nc.dram_tensor("xq_dbg", [128, T * D], BF16, kind="ExternalOutput").ap()
        dma(xqd_h, xq_fm[:], deps=[xq1_ready], sem="St")
    hq_im = [lnb_ref[0][:, 0:512], lnb_ref[0][:, 512:1024]]

    _, ffn1_done, slab_done = ffn_phase(
        x_res, xq_fm, h_buf, wgt_g, wut_g, wdt_g, hq_im,
        xq1_ready, None, out_dram=None,
        wq_direct=(wraw, wtm, stat_prev[0]), IGF=IG1)

    p.fence()
    rs1c.close()

    lnx2_es = contextlib.ExitStack()
    lnx2_fm = lnx2_es.enter_context(
        nc.sbuf_tensor("lnx2_fm", [128, FC * S], BF16, side="left"))
    sq_ln2 = lnx2_es.enter_context(
        nc.sbuf_tensor("sq_ln2", [128, D], F32, side="left"))
    stat_dump[0] = sq_ln2[:]
    spill_toks = []
    lnb = lnb_ref[0]
    st_last = None
    for t in range(T):
        st_last = ln_stats([ffn1_done], x_res[:, t * D:(t + 1) * D], t)
        spill_toks.append(dma(x1_d[t * 128:(t + 1) * 128, :],
                              x_res[:, t * D:(t + 1) * D],
                              deps=[ffn1_done], sem="St"))
    t_nb2 = ln_finalize([st_last])
    ln2_last = None
    ln2_tr = {0: None, 1: None}
    for t in range(T):
        half = t % 2
        lslice = lnb[:, half * D:(half + 1) * D]
        aff = p.emit("scalar", lambda e, tt=t, ls=lslice: e.activation(
            out=ls, in_=x_res[:, tt * D:(tt + 1) * D], func=AF.Identity,
            scale=st_rstd[:, tt:tt + 1], bias=st_nbias[:, tt:tt + 1]),
            deps=[t_nb2, ln2_tr[half]], inc="AC", chain=False)
        par = t % 2
        tok = transpose_batch(
            [lslice[:, fc * 128:(fc + 1) * 128] for fc in range(FC)],
            par, deps=[aff])
        ln2_tr[half] = tok
        ev = None
        for fc in range(FC):
            ev = p.emit("vector", lambda e, f=fc, tt=t, pr=par: e.tensor_copy(
                out=lnx2_fm[:, f * S + tt * 128: f * S + tt * 128 + 128],
                in_=ps_tr[pr][:, f * 128:(f + 1) * 128]),
                deps=[tok] if fc == 0 else (), inc="DV", chain=False)
        tr_free[par] = ev
        ln2_last = ev
    lnx2_ready = ln2_last
    if DBG and (DBGMASK & 8):
        lnx2d_h = nc.dram_tensor("lnx2_dbg", [128, FC * S], BF16, kind="ExternalOutput").ap()
        rstd_h = nc.dram_tensor("rstd_dbg", [128, T], F32, kind="ExternalOutput").ap()
        dma(lnx2d_h, lnx2_fm[:], deps=[lnx2_ready], sem="St")
        dma(rstd_h, st_rstd[:], deps=[lnx2_ready], sem="St")

    p.fence()
    rs1b.close()
    rs1.close()
    rsA = contextlib.ExitStack()
    k_fm = sbR(rsA, "k_fm", [128, FC * S], BF16)
    v_tm = sbR(rsA, "v_tm", [128, T * D], BF16)
    q_fm = sbR(rsA, "q_fm", [128, FC * S], BF16)
    rsAw = contextlib.ExitStack()
    wq_raw = sbR(rsAw, "wq_raw", [128, FC * 512], F32)
    wq_bf = sbR(rsAw, "wq_bf", [128, FC * 512], BF16)

    qkv_ev = None
    wq_read = {}
    wcast_done = {}
    for nck in range(6):
        t = []
        for fc in range(FC):
            t.append(dma(wq_raw[:, fc * 512:(fc + 1) * 512],
                    wqkv_h[fc * 128:(fc + 1) * 128, nck * 512:(nck + 1) * 512],
                    deps=[wcast_done.get(nck - 1)]))
        t_cast = p.emit("vector", lambda e: e.tensor_copy(
            out=wq_bf[:], in_=wq_raw[:]),
            deps=[t, wq_read.get(nck - 1)], inc="DV", chain=False)
        wcast_done[nck] = t_cast
        if nck < 4:
            dst = q_fm if nck < 2 else k_fm
            bias_off = 0 if nck < 2 else FC
            for nsub in range(4):
                dq = (nck % 2) * 4 + nsub
                for tc in range(4):
                    mi = (nsub * 4 + tc) % 4
                    pp = ps_mm[mi]
                    tok = None
                    for fc in range(FC):
                        tok = p.emit("tensor", lambda e, f=fc, ns=nsub, tcc=tc, ppp=pp: e.matmul(
                            ppp[:],
                            wq_bf[:, f * 512 + ns * 128: f * 512 + ns * 128 + 128],
                            lnx2_fm[:, f * S + tcc * 512: f * S + (tcc + 1) * 512],
                            start=(f == 0), stop=(f == FC - 1)),
                            deps=[t_cast, lnx2_ready, mm_free[mi]] if fc == 0 else (),
                            inc="PE")
                    wq_read[nck] = tok
                    qkv_ev = p.emit("scalar", lambda e, d_=dst, dqq=dq, tcc=tc,
                                    bo=bias_off, ppp=pp: e.activation(
                        out=d_[:, dqq * S + tcc * 512: dqq * S + (tcc + 1) * 512],
                        in_=ppp[:], func=AF.Identity,
                        bias=bqkv_t[:, bo + dqq: bo + dqq + 1]),
                        deps=[tok, bias_ready], inc="AC", chain=False)
                    mm_free[mi] = qkv_ev
        else:
            dvoff = (nck - 4) * 512
            for t_ in range(T):
                mi = t_ % 4
                pp = ps_mm[mi]
                tok = None
                for fc in range(FC):
                    tok = p.emit("tensor", lambda e, f=fc, tt=t_, ppp=pp: e.matmul(
                        ppp[:],
                        lnx2_fm[:, f * S + tt * 128: f * S + tt * 128 + 128],
                        wq_bf[:, f * 512:(f + 1) * 512],
                        start=(f == 0), stop=(f == FC - 1)),
                        deps=[t_cast, lnx2_ready, mm_free[mi]] if fc == 0 else (),
                        inc="PE")
                wq_read[nck] = tok
                qkv_ev = p.emit("scalar", lambda e, tt=t_, dv=dvoff, ppp=pp: e.activation(
                    out=v_tm[:, tt * D + dv: tt * D + dv + 512], in_=ppp[:],
                    func=AF.Identity), deps=[tok], inc="AC", chain=False)
                mm_free[mi] = qkv_ev
    qkv_done = qkv_ev
    if DBG and (DBGMASK & 4):
        qd_h = nc.dram_tensor("q_dbg", [128, FC * S], BF16, kind="ExternalOutput").ap()
        kd_h = nc.dram_tensor("k_dbg", [128, FC * S], BF16, kind="ExternalOutput").ap()
        vd_h = nc.dram_tensor("v_dbg", [128, T * D], BF16, kind="ExternalOutput").ap()
        dma(qd_h, q_fm[:], deps=[qkv_done], sem="St")
        dma(kd_h, k_fm[:], deps=[qkv_done], sem="St")
        dma(vd_h, v_tm[:], deps=[qkv_done], sem="St")

    p.fence()
    rsAw.close()
    lnx2_es.close()
    sq_es.close()
    rsB = contextlib.ExitStack()
    wo_raw = sbR(rsB, "wo_raw", [128, FC * 256], F32)
    wo_bf = sbR(rsB, "wo_bf", [128, FC * D], BF16)
    s_sb = sbR(rsB, "s_sb", [128, S], F32)
    p_sb = sbR(rsB, "p_sb", [128, S], BF16)
    pT_sb = sbR(rsB, "pT_sb", [128, T * 512], BF16)
    ctx_sb = sbR(rsB, "ctx_sb", [128, FC * 512], BF16)
    o_sb = sbR(rsB, "o_sb", [128, FC * 512], BF16)
    x1_t = sbR(rsB, "x1_t", [128, D], F32)
    sm_m = sbR(rsB, "sm_m", [128, 8], F32)

    t_wo = None
    for rnd in range(4):
        t = []
        for fc in range(FC):
            t.append(dma(wo_raw[:, fc * 256:(fc + 1) * 256],
                    wout_h[fc * 128:(fc + 1) * 128, rnd * 256:(rnd + 1) * 256],
                    deps=[t_wo]))
        half = rnd // 2
        off = (rnd % 2) * 256
        t_wo = p.emit("vector", lambda e, h_=half, o_=off: e.tensor_copy(
            out=wo_bf[:].rearrange("p (h f w) -> p h f w", h=2, f=FC)[
                :, h_, :, o_:o_ + 256],
            in_=wo_raw[:].rearrange("p (f w) -> p f w", w=256)),
            deps=[t], inc="DV", chain=False)

    x2_toks = [None] * T
    x1_load = {}
    if DBG and (DBGMASK & 16):
        pd_h = nc.dram_tensor("p_dbg", [128, S], BF16, kind="ExternalOutput").ap()
        sd_h = nc.dram_tensor("s_dbg", [128, S], F32, kind="ExternalOutput").ap()
        smd_h = nc.dram_tensor("sm_dbg", [128, 8], F32, kind="ExternalOutput").ap()
        ctxd_h = nc.dram_tensor("ctx_dbg", [128, FC * 512], BF16, kind="ExternalOutput").ap()
        od_h = nc.dram_tensor("o_dbg", [128, FC * 512], BF16, kind="ExternalOutput").ap()
    for qg in range(4):
        pT_evs = []
        tr_toks = {}
        for k in range(4):
            qt = qg * 4 + k
            ev_toks = []
            for kc in range(4):
                pp = ps_mm[kc]
                tok = None
                for fc in range(FC):
                    tok = p.emit("tensor", lambda e, f=fc, q_=qt, kcc=kc, ppp=pp: e.matmul(
                        ppp[:],
                        q_fm[:, f * S + q_ * 128: f * S + q_ * 128 + 128],
                        k_fm[:, f * S + kcc * 512: f * S + (kcc + 1) * 512],
                        start=(f == 0), stop=(f == FC - 1)),
                        deps=[qkv_done, mm_free[kc]] if fc == 0 else (), inc="PE")
                ev = p.emit("vector", lambda e, kcc=kc, ppp=pp: e.tensor_copy(
                    out=s_sb[:, kcc * 512:(kcc + 1) * 512], in_=ppp[:]),
                    deps=[tok, (("AC", p.cnt["AC"]) if (qg or k) else None)],
                    inc="DV", chain=False)
                ev_toks.append(ev)
                mm_free[kc] = ev
            t_m = p.emit("vector", lambda e: e.tensor_reduce(
                out=sm_m[:, 0:1], in_=s_sb[:], axis=AX.X, op=ALU.max, negate=True),
                deps=ev_toks, inc="DV")
            t_m = p.emit("vector", lambda e: e.tensor_scalar(
                out=sm_m[:, 0:1], in0=sm_m[:, 0:1], scalar1=1.0 / 32.0,
                scalar2=None, op0=ALU.mult), inc="DV")
            t_e = p.emit("scalar", lambda e, kk=k: e.activation(
                out=p_sb[:], in_=s_sb[:], func=AF.Exp, scale=1.0 / 32.0,
                bias=sm_m[:, 0:1], accum_out=sm_m[:, 1 + kk:2 + kk]),
                deps=[t_m, tr_toks.get(k - 1)], inc="AC", chain=False)
            t_r = p.emit("scalar", lambda e, kk=k: e.activation(
                out=sm_m[:, 5:6], in_=sm_m[:, 1 + kk:2 + kk], func=AF.Ln), inc="AC")
            t_r = p.emit("scalar", lambda e: e.activation(
                out=sm_m[:, 5:6], in_=sm_m[:, 5:6], func=AF.Exp, scale=-1.0),
                inc="AC")
            t_pn = p.emit("vector", lambda e: e.tensor_scalar(
                out=p_sb[:], in0=p_sb[:], scalar1=sm_m[:, 5:6], scalar2=None,
                op0=ALU.mult), deps=[t_r], inc="DV", chain=False)
            if DBG and (DBGMASK & 16) and qg == 0 and k == 0:
                dma(pd_h, p_sb[:], deps=[t_pn], sem="St")
                dma(sd_h, s_sb[:], deps=[t_pn], sem="St")
                dma(smd_h, sm_m[:], deps=[t_pn], sem="St")
            for half in range(2):
                par = half
                tok = transpose_batch(
                    [p_sb[:, (half * 8 + kt) * 128:(half * 8 + kt) * 128 + 128]
                     for kt in range(8)], par, deps=[t_pn])
                tr_toks[k] = tok
                ev = tr_evac("vector", par, lambda e, h_=half, kk=k, pr=par: e.tensor_copy(
                    out=pT_sb[:].rearrange("p (kt w) -> p kt w", w=512)[
                        :, h_ * 8:(h_ + 1) * 8, kk * 128:(kk + 1) * 128],
                    in_=ps_tr[pr][:].rearrange("p (b w) -> p b w", w=128)),
                    deps=[tok])
                pT_evs.append(ev)
        av_ev = None
        for dc in range(FC):
            mi = dc % 4
            pp = ps_mm[mi]
            tok = None
            for kt in range(T):
                tok = p.emit("tensor", lambda e, ktt=kt, d_=dc, ppp=pp: e.matmul(
                    ppp[:],
                    v_tm[:, ktt * D + d_ * 128: ktt * D + d_ * 128 + 128],
                    pT_sb[:, ktt * 512:(ktt + 1) * 512],
                    start=(ktt == 0), stop=(ktt == T - 1)),
                    deps=(pT_evs + [mm_free[mi]]) if kt == 0 else (), inc="PE")
            av_ev = p.emit("scalar", lambda e, d_=dc, ppp=pp: e.activation(
                out=ctx_sb[:, d_ * 512:(d_ + 1) * 512], in_=ppp[:],
                func=AF.Identity,
                bias=bqkv_t[:, 2 * FC + d_: 2 * FC + d_ + 1]),
                deps=[tok], inc="AC", chain=False)
            mm_free[mi] = av_ev
        oj_ev = None
        for oc in range(FC):
            mi = oc % 4
            pp = ps_mm[mi]
            woff = (oc // 4) * FC * 512 + (oc % 4) * 128
            tok = None
            for dc in range(FC):
                tok = p.emit("tensor", lambda e, d_=dc, ppp=pp, wo=woff: e.matmul(
                    ppp[:], wo_bf[:, d_ * 512 + wo: d_ * 512 + wo + 128],
                    ctx_sb[:, d_ * 512:(d_ + 1) * 512],
                    start=(d_ == 0), stop=(d_ == FC - 1)),
                    deps=[av_ev, t_wo, mm_free[mi]] if dc == 0 else (), inc="PE")
            oj_ev = p.emit("scalar", lambda e, o_=oc, ppp=pp: e.activation(
                out=o_sb[:, o_ * 512:(o_ + 1) * 512], in_=ppp[:],
                func=AF.Identity, bias=bout_t[:, o_: o_ + 1]),
                deps=[tok], inc="AC", chain=False)
            mm_free[mi] = oj_ev
        if DBG and (DBGMASK & 16) and qg == 0:
            dma(ctxd_h, ctx_sb[:], deps=[av_ev], sem="St")
            dma(od_h, o_sb[:], deps=[oj_ev], sem="St")
        for k in range(4):
            t_abs = qg * 4 + k
            x1_load[t_abs] = dma(
                x1_t[:],
                x1_d[t_abs * 128:(t_abs + 1) * 128, :],
                deps=[spill_toks[t_abs], x2_toks[t_abs - 1] if t_abs >= 1 else None])
            par = k % 2
            tok = transpose_batch(
                [o_sb[:, oc * 512 + k * 128: oc * 512 + k * 128 + 128]
                 for oc in range(FC)], par, deps=[oj_ev])
            t_add = tr_evac("vector", par, lambda e, pr=par: e.tensor_tensor(
                out=x1_t[:], in0=x1_t[:], in1=ps_tr[pr][:], op=ALU.add),
                deps=[tok, x1_load[t_abs]])
            x2_toks[t_abs] = dma(x2_d[t_abs * 128:(t_abs + 1) * 128, :],
                                 x1_t[:], deps=[t_add], sem="St")

    p.fence()
    rsB.close()
    rsA.close()
    rs2 = contextlib.ExitStack()
    x_res2 = sbR(rs2, "x_res2", [128, T * D], F32)
    lnb_ref[0] = sbR(rs2, "lnb2", [128, 2 * D], BF16)
    xq_fm2 = sbR(rs2, "xq_fm2", [128, T * D], BF16)
    h_buf2 = [sbR(rs2, "h2_0", [128, T * IG1], BF16),
              sbR(rs2, "h2_1", [128, T * IG1], BF16)]
    wgt_g2 = [sbR(rs2, "wgt2_0", [128, FC * IG1], BF16),
              sbR(rs2, "wgt2_1", [128, FC * IG1], BF16)]
    wut_g2 = [sbR(rs2, "wut2_0", [128, FC * IG1], BF16),
              sbR(rs2, "wut2_1", [128, FC * IG1], BF16)]
    wdt_g2 = [sbR(rs2, "wdt2_0", [128, 2 * D], BF16),
              sbR(rs2, "wdt2_1", [128, 2 * D], BF16)]
    hq_im2 = [lnb_ref[0][:, 0:512], lnb_ref[0][:, 512:1024]]
    sq_sc2 = sbR(rs2, "sq_sc2", [128, D], F32)
    stat_dump[0] = sq_sc2[:]

    _, xq2_ready = lnq_phase(x2_d, x_res2, xq_fm2, x2_toks, None)
    out_toks, _, _ = ffn_phase(x_res2, xq_fm2, h_buf2, wgt_g2, wut_g2, wdt_g2,
                               hq_im2, xq2_ready, slab_done, out_dram=y_h,
                               IGF=IG1)

    p.emit("sync", lambda e: e.nop(), deps=[st_fence_toks()], inc=None)

    p.check()
    with nc.Block() as block:
        p.realize(block)
    rs2.close()
    es.close()
    return nc


_NC_CACHE = {}


def kernel(L0_attn_x, L0_attn_W_qkv, L0_attn_b_qkv, L0_attn_W_out, L0_attn_b_out,
           L0_ffn_w_gate, L0_ffn_w_up, L0_ffn_w_down):
    B, S, D = L0_attn_x.shape
    key = (S, D)
    if key not in _NC_CACHE:
        _NC_CACHE[key] = build_nc(S=S, D=D, I=L0_ffn_w_gate.shape[0])
    nc = _NC_CACHE[key]
    wqkv = np.ascontiguousarray(L0_attn_W_qkv, dtype=np.float32)
    bqkv = np.ascontiguousarray(L0_attn_b_qkv, dtype=np.float32)
    wout = np.ascontiguousarray(L0_attn_W_out, dtype=np.float32)
    bout = np.ascontiguousarray(L0_attn_b_out, dtype=np.float32)
    wg = np.ascontiguousarray(L0_ffn_w_gate, dtype=np.float32)
    wu = np.ascontiguousarray(L0_ffn_w_up, dtype=np.float32)
    wd = np.ascontiguousarray(L0_ffn_w_down, dtype=np.float32)
    in_maps = []
    for i in range(B):
        in_maps.append({
            "x": np.ascontiguousarray(L0_attn_x[i], dtype=np.float32),
            "wqkv": wqkv, "bqkv": bqkv, "wout": wout, "bout": bout,
            "wg": wg, "wu": wu, "wd": wd,
        })
    res = run_bass_kernel_spmd(nc, in_maps, core_ids=list(range(B)))
    out = np.stack([np.asarray(r["y"]) for r in res.results], axis=0)
    return out.astype(np.float32)


# version 26
# speedup vs baseline: 1.1558x; 1.0177x over previous
import contextlib
import numpy as np

import concourse.bass as bass
import concourse.mybir as mybir
from concourse.bass_utils import run_bass_kernel_spmd

F32 = mybir.dt.float32
BF16 = mybir.dt.bfloat16
ALU = mybir.AluOpType
AF = mybir.ActivationFunctionType
AX = mybir.AxisListType

ENGS = ("sync", "scalar", "vector", "gpsimd", "tensor")
LN6 = float(np.log(np.float32(6.0)))
INV12 = float(np.float32(1.0 / 12.0))


class Prog:

    def __init__(self, nc):
        self.nc = nc
        self.q = {e: [] for e in ENGS}
        self.cnt = {}
        self.seen = {e: {} for e in ENGS}
        self.sems = {}
        self.fence_toks = []
        self.last_chained = {}

    def sem(self, es, name):
        self.sems[name] = es.enter_context(self.nc.semaphore(name))
        self.cnt[name] = 0

    def tok(self, sem):
        return (sem, self.cnt[sem])

    def fence(self):
        self.fence_toks = [(s, c) for s, c in self.cnt.items() if c > 0]

    OWN = {"vector": "DV", "scalar": "AC", "gpsimd": "PO"}

    def emit(self, eng, fn, deps=(), inc=None, inc_by=1, chain=True):
        w = []
        flat = []
        own = self.OWN.get(eng)
        if own is not None and self.cnt[own] > 0 and (chain or self.last_chained.get(eng, True)):
            flat.append((own, self.cnt[own]))
        self.last_chained[eng] = chain
        for d in list(deps) + self.fence_toks:
            if d is None:
                continue
            if isinstance(d, list):
                flat.extend(x for x in d if x is not None)
            else:
                flat.append(d)
        for d in flat:
            s, v = d
            if v <= 0 or self.seen[eng].get(s, 0) >= v:
                continue
            self.seen[eng][s] = v
            w.append((s, v))
        tok = None
        rec_inc = None
        if inc is not None:
            self.cnt[inc] += inc_by
            tok = (inc, self.cnt[inc])
            if eng not in self.OWN:
                self.seen[eng][inc] = self.cnt[inc]
            rec_inc = (inc, inc_by)
        self.q[eng].append((fn, w, rec_inc))
        return tok

    def check(self):
        ptr = {e: 0 for e in ENGS}
        cnt = {s: 0 for s in self.cnt}
        progress = True
        while progress:
            progress = False
            for e in ENGS:
                while ptr[e] < len(self.q[e]):
                    fn, waits, inc = self.q[e][ptr[e]]
                    if any(cnt[s] < v for s, v in waits):
                        break
                    if inc is not None:
                        cnt[inc[0]] += inc[1]
                    ptr[e] += 1
                    progress = True
        stuck = {e: (len(self.q[e]) - ptr[e]) for e in ENGS if ptr[e] < len(self.q[e])}
        if stuck:
            detail = {}
            for e in stuck:
                fn, waits, inc = self.q[e][ptr[e]]
                detail[e] = [(s, v, cnt[s]) for s, v in waits if cnt[s] < v]
            raise RuntimeError(f"deadlock: stuck ops {stuck}; heads wait {detail}")

    def realize(self, block):
        sems = self.sems

        def runner(eng_name):
            ops = self.q[eng_name]

            def run(eng):
                for fn, waits, inc in ops:
                    for s, v in waits:
                        eng.wait_ge(sems[s], v)
                    inst = fn(eng)
                    if inc is not None:
                        inst.then_inc(sems[inc[0]], inc[1])
            return run

        block.sync(runner("sync"))
        block.scalar(runner("scalar"))
        block.vector(runner("vector"))
        block.gpsimd(runner("gpsimd"))
        block.tensor(runner("tensor"))


def cdiv(a, b):
    return (a + b - 1) // b


def build_nc(S=2048, D=1024, I=2816, DBG=False, FFN_FENCE=False, LNQ_FENCE=False, EXT_KIND=False, DBGMASK=31):
    globals()["_FFN_FENCE"] = FFN_FENCE
    globals()["_LNQ_FENCE"] = LNQ_FENCE
    T = S // 128
    FC = D // 128
    IG = 512
    NG = cdiv(I, IG)
    QW = 1024

    nc = bass.Bass("TRN2", target_bir_lowering=False, detect_race_conditions=False)

    x_h = nc.dram_tensor("x", [S, D], F32, kind="ExternalInput").ap()
    wqkv_h = nc.dram_tensor("wqkv", [D, 3 * D], F32, kind="ExternalInput").ap()
    bqkv_h = nc.dram_tensor("bqkv", [3 * D], F32, kind="ExternalInput").ap()
    wout_h = nc.dram_tensor("wout", [D, D], F32, kind="ExternalInput").ap()
    bout_h = nc.dram_tensor("bout", [D], F32, kind="ExternalInput").ap()
    wg_h = nc.dram_tensor("wg", [I, D], F32, kind="ExternalInput").ap()
    wu_h = nc.dram_tensor("wu", [I, D], F32, kind="ExternalInput").ap()
    wd_h = nc.dram_tensor("wd", [D, I], F32, kind="ExternalInput").ap()
    y_h = nc.dram_tensor("y", [S, D], F32, kind="ExternalOutput").ap()
    wgT_d = nc.dram_tensor("wgT_d", [D, I], BF16, kind="ExternalOutput" if (DBG or EXT_KIND) else "Internal").ap()
    wuT_d = nc.dram_tensor("wuT_d", [D, I], BF16, kind="ExternalOutput" if (DBG or EXT_KIND) else "Internal").ap()
    wdT_d = nc.dram_tensor("wdT_d", [I, D], BF16, kind="ExternalOutput" if (DBG or EXT_KIND) else "Internal").ap()
    x1_d = nc.dram_tensor("x1_d", [S, D], F32, kind="ExternalOutput" if (DBG or EXT_KIND) else "Internal").ap()
    x2_d = nc.dram_tensor("x2_d", [S, D], F32, kind="ExternalOutput" if (DBG or EXT_KIND) else "Internal").ap()

    es = contextlib.ExitStack()

    def sbL(name, shape, dt):
        return es.enter_context(nc.sbuf_tensor(name, shape, dt, side="left"))

    def pst(name, shape, dt):
        return es.enter_context(nc.psum_tensor(name, shape, dt))

    p = Prog(nc)
    for s in ("PE", "DV", "AC", "PO"):
        p.sem(es, s)
    NLD, NST = 16, 8
    for i in range(NLD):
        p.sem(es, f"L{i}")
    for i in range(NST):
        p.sem(es, f"S{i}")
    rotor = {"L": 0, "S": 0}

    def st_fence_toks():
        return [(f"S{i}", p.cnt[f"S{i}"]) for i in range(NST)]

    def ld_fence_toks():
        return [(f"L{i}", p.cnt[f"L{i}"]) for i in range(NLD)]

    ident_b = sbL("ident_b", [128, 128], BF16)
    ident_f = sbL("ident_f", [128, 128], F32)
    bqkv_t = sbL("bqkv_t", [128, 3 * FC], F32)
    bout_t = sbL("bout_t", [128, FC], F32)
    c_ln6 = sbL("c_ln6", [128, 1], F32)
    c_eps = sbL("c_eps", [128, 1], F32)
    nc.const_aps.register(F32, LN6, c_ln6[:])
    nc.const_aps.register(F32, 1e-5, c_eps[:])
    MIDS = (0.25, 0.75, 1.25, 1.75, 2.5, 3.5, 5.0)
    c_mid = []
    for i, m in enumerate(MIDS):
        t = sbL(f"c_mid{i}", [128, 1], F32)
        nc.const_aps.register(F32, -float(m), t[:])
        c_mid.append(t)
    st_sum = sbL("st_sum", [128, T], F32)
    st_sq = sbL("st_sq", [128, T], F32)
    st_rstd = sbL("st_rstd", [128, T], F32)
    st_nbias = sbL("st_nbias", [128, T], F32)
    lnb_ref = [None]
    stat_dump = [None]
    stat_prev = [None]
    wq_vf = [None, None, None]

    class QSet:
        def __init__(self, i):
            self.i = i
            self.v = sbL(f"qs{i}_v", [128, QW], BF16)
            self.a = sbL(f"qs{i}_a", [128, QW], BF16)
            self.b = sbL(f"qs{i}_b", [128, QW], BF16)
            self.c = sbL(f"qs{i}_c", [128, QW], BF16)
            self.d = sbL(f"qs{i}_d", [128, QW], BF16)
            self.g = sbL(f"qs{i}_g", [128, QW], BF16)
            self.bm = sbL(f"qs{i}_bm", [128, QW // 16], F32)
            self.iv = sbL(f"qs{i}_iv", [128, QW // 16], F32)
            self.sh = sbL(f"qs{i}_sh", [128, QW // 16], F32)
            self.out = None
            self.ivtok = None
            self.ztok = None
    qsets = [QSet(0), QSet(1), QSet(2)]
    qrot = [0]
    wqrot = [0]
    sq_es = contextlib.ExitStack()

    rs1 = contextlib.ExitStack()

    def sbR(stk, name, shape, dt):
        return stk.enter_context(nc.sbuf_tensor(name, shape, dt, side="right"))

    x_res = sbR(rs1, "x_res", [128, T * D], F32)
    rs1b = contextlib.ExitStack()
    xq_fm = sbR(rs1b, "xq_fm", [128, T * D], BF16)
    lnb_ref[0] = sbR(rs1b, "lnb1", [128, 2 * D], BF16)
    rs1c = contextlib.ExitStack()
    wq_vf[0] = sbR(rs1c, "wqvf0", [128, QW], F32)
    wq_vf[1] = sbR(rs1c, "wqvf1", [128, QW], F32)
    wraw = sbR(rs1c, "wraw", [128, 2 * D], F32)
    wtm = [sbR(rs1c, "wtm0", [128, 2 * D], BF16),
           sbR(rs1c, "wtm1", [128, 2 * D], BF16),
           sbR(rs1c, "wtm2", [128, 2 * D], BF16)]
    IG1 = 256
    h_buf = [sbR(rs1c, "h_buf0", [128, T * IG1], BF16),
             sbR(rs1c, "h_buf1", [128, T * IG1], BF16)]
    wgt_g = [sbR(rs1c, "wgt_g0", [128, FC * IG1], BF16),
             sbR(rs1c, "wgt_g1", [128, FC * IG1], BF16)]
    wut_g = [sbR(rs1c, "wut_g0", [128, FC * IG1], BF16),
             sbR(rs1c, "wut_g1", [128, FC * IG1], BF16)]
    wdt_g = [sbR(rs1c, "wdt_g0", [128, 2 * D], BF16),
             sbR(rs1c, "wdt_g1", [128, 2 * D], BF16)]
    stat_dump[0] = wq_vf[0][:, 0:1024]
    braw = wq_vf[0][0:32, 0:128]


    ps_mm = [pst(f"ps_mm{i}", [128, 512], F32) for i in range(4)]
    ps_y = [pst(f"ps_y{i}", [128, 512], F32) for i in range(2)]
    ps_tr = [pst(f"ps_tr{i}", [128, 1024], BF16) for i in range(2)]

    tr_free = {0: None, 1: None}
    mm_free = {i: None for i in range(4)}
    y_free = {0: None, 1: None}

    def dma(out, in_, deps=(), sem="Ld"):
        pool, n = ("L", NLD) if sem == "Ld" else ("S", NST)
        s = f"{pool}{rotor[pool]}"
        rotor[pool] = (rotor[pool] + 1) % n
        prev = (s, p.cnt[s])
        return p.emit("sync", lambda e, o=out, i=in_: e.dma_start(out=o, in_=i),
                      deps=list(deps) + [prev], inc=s, inc_by=16)

    def transpose_batch(blocks, parity, deps=()):
        pt = ps_tr[parity]
        tok = None
        for k, src in enumerate(blocks):
            d = (list(deps) + [tr_free[parity]]) if k == 0 else ()
            tok = p.emit("tensor", lambda e, s=src, kk=k: e.transpose(
                pt[:, kk * 128:(kk + 1) * 128], s, ident_b[:]), deps=d, inc="PE")
        return tok

    def tr_evac(eng, parity, fn, deps=(), inc="DV"):
        tok = p.emit(eng, fn, deps=deps, inc=inc, chain=False)
        tr_free[parity] = tok
        return tok

    def tr_evac_act(parity, out_ap, w, deps=()):
        tok = p.emit("scalar", lambda e, o=out_ap, pr=parity, w_=w:
                     e.activation(out=o, in_=ps_tr[pr][:, 0:w_],
                                  func=AF.Identity),
                     deps=deps, inc="AC", chain=False)
        tr_free[parity] = tok
        return tok

    def quant_a(src_ap, dst_ap, width, deps, src_f32=False, act_levels=3,
                out_war=None, sh_scale=INV12):
        if src_f32:
            S = qsets[wqrot[0] % 2]
            wqrot[0] += 1
        else:
            S = qsets[qrot[0] % len(qsets)]
            qrot[0] += 1
        nb = width // 16
        v = (wq_vf[S.i] if src_f32 else S.v)[:, 0:width]
        bm, iv, sh = S.bm[:, 0:nb], S.iv[:, 0:nb], S.sh[:, 0:nb]
        g = S.g[:, 0:width]
        prev_out, prev_iv = S.out, S.ivtok
        deps = list(deps)
        t_ax = p.emit("scalar", lambda e: e.activation(
            out=v, in_=src_ap, func=AF.Abs), deps=deps + [prev_out],
            inc="AC", chain=False)
        v3 = v.rearrange("p (n b) -> p n b", b=16)
        p.emit("vector", lambda e: e.tensor_reduce(
            out=bm, in_=v3, axis=AX.X, op=ALU.max),
            deps=[t_ax, prev_iv, prev_out], inc="DV")
        t_bc = p.emit("vector", lambda e: e.tensor_scalar(
            out=bm, in0=bm, scalar1=6e-12, scalar2=None, op0=ALU.max), inc="DV")
        t_sh = p.emit("vector", lambda e: e.tensor_scalar(
            out=sh, in0=bm, scalar1=sh_scale, scalar2=None, op0=ALU.mult), inc="DV")
        p.emit("scalar", lambda e: e.activation(
            out=iv, in_=bm, func=AF.Ln), deps=[t_bc], inc="AC")
        t_iv = p.emit("scalar", lambda e: e.activation(
            out=iv, in_=iv, func=AF.Exp, scale=-1.0, bias=LN6), inc="AC")
        t_sg = p.emit("scalar", lambda e: e.activation(
            out=g, in_=src_ap, func=AF.Sign), deps=deps + [prev_out],
            inc="AC", chain=False)
        iv3 = iv.unsqueeze(2).broadcast_to([128, nb, 16])
        t_v = p.emit("gpsimd", lambda e: e.tensor_tensor(
            out=v3, in0=v3, in1=iv3, op=ALU.mult),
            deps=[t_iv, t_ax, prev_out], inc="PO", chain=False)
        g3 = g.rearrange("p (n b) -> p n b", b=16)
        sh3 = sh.unsqueeze(2).broadcast_to([128, nb, 16])
        t_ssg = p.emit("gpsimd", lambda e: e.tensor_tensor(
            out=g3, in0=g3, in1=sh3, op=ALU.mult), deps=[t_sg, t_sh],
            inc="PO", chain=False)
        S.ivtok = t_iv
        return dict(S=S, v=v, g=g, width=width, dst=dst_ap, t_v=t_v,
                    t_ssg=t_ssg, act_levels=act_levels, out_war=out_war,
                    t_ax=t_ax, t_sg=t_sg, ztok_war=S.ztok)

    def quant_b(st):
        S, v, g, width = st["S"], st["v"], st["g"], st["width"]
        A, B, C, Dd = (S.a[:, 0:width], S.b[:, 0:width], S.c[:, 0:width],
                       S.d[:, 0:width])
        t_v = st["t_v"]
        po = S.out
        qtok = st.pop("ztok_war")
        p.emit("vector", lambda e: e.tensor_scalar(
            out=A, in0=v, scalar1=0.25, scalar2=None, op0=ALU.is_gt),
            deps=[t_v], inc="DV", chain=False)
        for mm in (0.75, 1.25):
            p.emit("vector", lambda e, m_=mm: e.scalar_tensor_tensor(
                out=A, in0=v, scalar=m_, in1=A, op0=ALU.is_gt, op1=ALU.add),
                inc="DV", chain=False)
        t3 = p.emit("vector", lambda e: e.scalar_tensor_tensor(
            out=A, in0=v, scalar=1.75, in1=A, op0=ALU.is_gt, op1=ALU.add),
            inc="DV", chain=False)
        s5 = p.emit("scalar", lambda e: e.activation(
            out=B, in_=v, func=AF.Sign, bias=-2.5), deps=[t_v, qtok],
            inc="AC", chain=False)
        s6 = p.emit("scalar", lambda e: e.activation(
            out=C, in_=v, func=AF.Sign, bias=-3.5), deps=[po],
            inc="AC", chain=False)
        s7 = p.emit("scalar", lambda e: e.activation(
            out=Dd, in_=v, func=AF.Sign, bias=-5.0),
            inc="AC", chain=False)
        w1 = p.emit("vector", lambda e: e.tensor_tensor(
            out=B, in0=B, in1=C, op=ALU.add), deps=[s5, s6],
            inc="DV", chain=False)
        w2 = p.emit("vector", lambda e: e.scalar_tensor_tensor(
            out=B, in0=Dd, scalar=2.0, in1=B, op0=ALU.mult, op1=ALU.add),
            deps=[s7, w1], inc="DV", chain=False)
        t_qh = p.emit("vector", lambda e: e.scalar_tensor_tensor(
            out=C, in0=B, scalar=4.0, in1=A, op0=ALU.add, op1=ALU.add),
            deps=[w1, t3], inc="DV", chain=False)
        S.ztok = t_qh
        t_out = p.emit("gpsimd", lambda e, d_=st["dst"], C_=C, g_=g: e.tensor_tensor(
            out=d_, in0=C_, in1=g_, op=ALU.mult),
            deps=[t_qh, st["t_ssg"], st["out_war"]], inc="PO", chain=False)
        S.out = t_out
        return t_out

    def quant(src_ap, dst_ap, width, deps, src_f32=False, act_levels=3):
        return quant_b(quant_a(src_ap, dst_ap, width, deps, src_f32, act_levels))

    def ln_stats(deps, src_tile_ap, col):
        sd = stat_dump[0]
        p.emit("scalar", lambda e, sd=sd: e.activation(
            out=sd, in_=src_tile_ap, func=AF.Identity,
            accum_out=st_sum[:, col:col + 1]),
            deps=list(deps) + [stat_prev[0]], inc="AC", chain=False)
        t_sq = p.emit("scalar", lambda e, sd=sd: e.activation(
            out=sd, in_=src_tile_ap, func=AF.Square,
            accum_out=st_sq[:, col:col + 1]),
            inc="AC", chain=False)
        stat_prev[0] = t_sq
        return t_sq

    def ln_finalize(deps, dbg=None):
        def dcp(slot, src, dep=None):
            if dbg is None:
                return None
            return p.emit("vector", lambda e: e.tensor_copy(
                out=dbg[:, slot * T:(slot + 1) * T], in_=src), deps=[dep] if dep else (), inc="DV")
        p.emit("vector", lambda e: e.tensor_scalar(
            out=st_sum[:], in0=st_sum[:], scalar1=1.0 / D, scalar2=None,
            op0=ALU.mult), deps=deps, inc="DV")
        p.emit("vector", lambda e: e.tensor_scalar(
            out=st_sq[:], in0=st_sq[:], scalar1=1.0 / D, scalar2=None,
            op0=ALU.mult), inc="DV")
        dcp(2, st_sq[:])
        p.emit("vector", lambda e: e.tensor_tensor(
            out=st_rstd[:], in0=st_sum[:], in1=st_sum[:], op=ALU.mult), inc="DV")
        dcp(3, st_rstd[:])
        t_var = p.emit("vector", lambda e: e.tensor_tensor(
            out=st_sq[:], in0=st_sq[:], in1=st_rstd[:], op=ALU.subtract), inc="DV")
        dcp(4, st_sq[:])
        t_ln = p.emit("scalar", lambda e: e.activation(
            out=st_rstd[:], in_=st_sq[:], func=AF.Ln, bias=1e-5),
            deps=[t_var], inc="AC")
        dcp(5, st_rstd[:], t_ln)
        t_r = p.emit("scalar", lambda e: e.activation(
            out=st_rstd[:], in_=st_rstd[:], func=AF.Exp, scale=-0.5),
            deps=[dbg and p.tok("DV")], inc="AC")
        dcp(6, st_rstd[:], t_r)
        return p.emit("vector", lambda e: e.scalar_tensor_tensor(
            out=st_nbias[:], in0=st_sum[:], scalar=-1.0, in1=st_rstd[:],
            op0=ALU.mult, op1=ALU.mult), deps=[t_r], inc="DV")

    p.emit("gpsimd", lambda e: e.memset(ident_b[:], 0.0), inc="PO")
    p.emit("gpsimd", lambda e: e.affine_select(
        out=ident_b[:], in_=ident_b[:], compare_op=ALU.not_equal, fill=1.0,
        base=0, pattern=[[-1, 128]], channel_multiplier=1), inc="PO")
    p.emit("gpsimd", lambda e: e.memset(ident_f[:], 0.0), inc="PO")
    t_idf = p.emit("gpsimd", lambda e: e.affine_select(
        out=ident_f[:], in_=ident_f[:], compare_op=ALU.not_equal, fill=1.0,
        base=0, pattern=[[-1, 128]], channel_multiplier=1), inc="PO")
    p.emit("gpsimd", lambda e: e.memset(c_ln6[:], LN6), inc="PO")
    p.emit("gpsimd", lambda e: e.memset(c_eps[:], 1e-5), inc="PO")
    for i, m in enumerate(MIDS):
        p.emit("gpsimd", lambda e, t=c_mid[i], mm=m: e.memset(t[:], -float(mm)), inc="PO")
    init_tok = p.tok("PO")
    p.fence()

    t_bq = dma(braw[0:3 * FC, :], bqkv_h.rearrange("(c q) -> c q", q=128))
    t_tbq = p.emit("tensor", lambda e: e.transpose(
        ps_y[0][:, 0:3 * FC], braw[0:3 * FC, :], ident_f[0:3 * FC, 0:3 * FC]),
        deps=[t_bq, t_idf], inc="PE")
    t_ebq = p.emit("vector", lambda e: e.tensor_copy(
        out=bqkv_t[:], in_=ps_y[0][:, 0:3 * FC]), deps=[t_tbq], inc="DV", chain=False)
    y_free[0] = t_ebq
    t_bo = dma(braw[0:FC, :], bout_h.rearrange("(c q) -> c q", q=128),
               deps=[t_ebq])
    t_tbo = p.emit("tensor", lambda e: e.transpose(
        ps_y[1][:, 0:FC], braw[0:FC, :], ident_f[0:FC, 0:FC]),
        deps=[t_bo], inc="PE")
    t_ebo = p.emit("vector", lambda e: e.tensor_copy(
        out=bout_t[:], in_=ps_y[1][:, 0:FC]), deps=[t_tbo], inc="DV", chain=False)
    y_free[1] = t_ebo
    bias_ready = t_ebo
    stat_prev[0] = t_ebo

    def lnq_phase(x_src_h, x_res, xq_fm, src_deps, dst_free, lnq_dbg=None):
        lnb = lnb_ref[0]
        """DMA x tiles into x_res, LayerNorm, fp4-quantize, transpose into
        xq_fm (tile-major fc blocks). Returns (x_toks, ready_token)."""
        x_toks = []
        for t in range(T):
            x_toks.append(dma(x_res[:, t * D:(t + 1) * D],
                              x_src_h[t * 128:(t + 1) * 128, :],
                              deps=[src_deps[t] if src_deps else None]))
        xq_t = [None] * T
        st_last = None
        for t in range(T):
            st_last = ln_stats([x_toks[t]], x_res[:, t * D:(t + 1) * D], t)
        if DBG and lnq_dbg:
            p.emit("vector", lambda e: e.tensor_copy(out=lnq_dbg[0][:, 0:T], in_=st_sum[:]),
                   deps=[st_last], inc="DV")
            st_last = p.emit("vector", lambda e: e.tensor_copy(out=lnq_dbg[0][:, T:2*T], in_=st_sq[:]),
                   deps=[st_last], inc="DV")
        t_nb = ln_finalize([st_last])
        if globals().get("_LNQ_FENCE"):
            p.fence()
        last = None
        lnb_tr = {0: None, 1: None}
        lnb_out = {0: None, 1: None}
        pend = None
        for t in range(T + 1):
            if t < T:
                half = t % 2
                lslice = lnb[:, half * D:(half + 1) * D]
                aff = p.emit("scalar", lambda e, tt=t, ls=lslice: e.activation(
                    out=ls, in_=x_res[:, tt * D:(tt + 1) * D], func=AF.Identity,
                    scale=st_rstd[:, tt:tt + 1], bias=st_nbias[:, tt:tt + 1]),
                    deps=[t_nb, lnb_out[half], lnb_tr[half]], inc="AC",
                    chain=False)
                st = quant_a(lslice, lslice, D, deps=[aff])
            if pend is not None:
                pst, pt = pend
                tok_out = quant_b(pst)
                phalf = pt % 2
                lnb_out[phalf] = tok_out
                pls = lnb[:, phalf * D:(phalf + 1) * D]
                par = pt % 2
                tok = transpose_batch(
                    [pls[:, fc * 128:(fc + 1) * 128] for fc in range(FC)],
                    par, deps=[tok_out])
                lnb_tr[phalf] = tok
                last = tr_evac("vector", par, lambda e, tt=pt, pr=par: e.tensor_copy(
                    out=xq_fm[:, tt * D:(tt + 1) * D], in_=ps_tr[pr][:, 0:D]),
                    deps=[tok, dst_free])
                xq_t[pt] = last
            if t < T:
                pend = (st, t)
        return x_toks, xq_t

    def ffn_phase(x_res, xq_fm, h_buf, wgt_g, wut_g, wdt_g, hq_im,
                  xq_ready, wsrc_ready, out_dram=None, wq_direct=None,
                  IGF=512):
        NGF = cdiv(I, IGF)
        wload = {}
        wread_done = {}
        hq_done = {}
        hT_done = {}
        silu_mul_done = {}
        yacc_done = {}
        dmm_done = {}
        out_toks = [None] * T
        wq_src_free = [[]]
        st_grp = {}
        ev_g = {}
        ev_u = {}
        wd_evac = {}
        hqc_done = {}
        tr_task = {}
        pend_wq = {}

        def wq_chains(gi):
            wraw, wtm, stats_gate = wq_direct
            i0 = gi * IGF
            isz = min(IGF, I - i0)
            if isz <= 0:
                return
            nrc = cdiv(isz, 128)
            pend_wq[gi] = []
            gate = [stats_gate] if gi == 0 else []
            for wi, which in enumerate(("g", "u", "d")):
                k = 3 * gi + wi
                wt = wtm[k % 3]
                ldt = []
                if which in ("g", "u"):
                    src_h = wg_h if which == "g" else wu_h
                    for rc in range(nrc):
                        ldt.append(dma(wraw[:, rc * D:(rc + 1) * D],
                                  src_h[i0 + rc * 128: i0 + rc * 128 + 128, :],
                                  deps=wq_src_free[0]))
                    width = nrc * D
                else:
                    for rc in range(FC):
                        ldt.append(dma(wraw[:, rc * isz:(rc + 1) * isz],
                                  wd_h[rc * 128:(rc + 1) * 128, i0:i0 + isz],
                                  deps=wq_src_free[0]))
                    width = FC * isz
                sts = []
                for off in range(0, width, QW):
                    w = min(QW, width - off)
                    sts.append(quant_a(wraw[:, off:off + w],
                                       wt[:, off:off + w], w,
                                       deps=[ldt] + gate, src_f32=True,
                                       act_levels=7,
                                       out_war=tr_task.get(k - 3),
                                       sh_scale=(INV12 * 0.5) if which == "d"
                                       else INV12))
                t_q = None
                for st in sts:
                    t_q = quant_b(st)
                wq_src_free[0] = [t for st in sts
                                  for t in (st["t_ax"], st["t_sg"])]
                pend_wq[gi].append((which, t_q, wt, isz, nrc))

        def wq_tails(gi):
            if gi not in pend_wq:
                return
            par = gi % 2
            i0 = gi * IGF
            war_gu = [wread_done.get(gi - 2)] + st_grp.get(gi - 2, [])
            war_d = [dmm_done.get((gi - 2, T - 1))] + st_grp.get(gi - 2, [])
            for wi, (which, t_q, wt, isz, nrc) in enumerate(pend_wq.pop(gi)):
                k = 3 * gi + wi
                tok = None
                if which in ("g", "u"):
                    gbuf = wgt_g[par] if which == "g" else wut_g[par]
                    dst_d = wgT_d if which == "g" else wuT_d
                    for fc in range(FC):
                        tp = fc % 2
                        tok = transpose_batch(
                            [wt[:, rc * D + fc * 128: rc * D + fc * 128 + 128]
                             for rc in range(nrc)], tp, deps=[t_q])
                        ev = tr_evac_act(
                            tp, gbuf[:, fc * IGF: fc * IGF + nrc * 128],
                            nrc * 128, deps=[tok] + war_gu)
                        (ev_g if which == "g" else ev_u).setdefault(
                            gi, {})[fc] = ev
                        stt = dma(dst_d[fc * 128:(fc + 1) * 128, i0:i0 + isz],
                                  gbuf[:, fc * IGF: fc * IGF + isz],
                                  deps=[ev], sem="St")
                        st_grp.setdefault(gi, []).append(stt)
                else:
                    dbuf = wdt_g[par]
                    for ib in range(nrc):
                        tp = ib % 2
                        tok = transpose_batch(
                            [wt[:, hc * isz + ib * 128: hc * isz + ib * 128 + 128]
                             for hc in range(FC)], tp, deps=[t_q])
                        ev = tr_evac_act(
                            tp, dbuf[:, ib * D:(ib + 1) * D], D,
                            deps=[tok] + war_d)
                        wd_evac.setdefault(gi, []).append(ev)
                        stt = dma(wdT_d[i0 + ib * 128: i0 + ib * 128 + 128, :],
                                  dbuf[:, ib * D:(ib + 1) * D], deps=[ev],
                                  sem="St")
                        st_grp.setdefault(gi, []).append(stt)
                tr_task[k] = tok

        def load_group(g):
            gi0 = g * IGF
            gsz = min(IGF, I - gi0)
            par = g % 2
            dep_gu = [wsrc_ready[g], wread_done.get(g - 2)]
            dep_d = [wsrc_ready[g], dmm_done.get((g - 2, T - 1))]
            for fc in range(FC):
                ev_g.setdefault(g, {})[fc] = dma(
                    wgt_g[par][:, fc * IGF: fc * IGF + gsz],
                    wgT_d[fc * 128:(fc + 1) * 128, gi0:gi0 + gsz], deps=dep_gu)
                ev_u.setdefault(g, {})[fc] = dma(
                    wut_g[par][:, fc * IGF: fc * IGF + gsz],
                    wuT_d[fc * 128:(fc + 1) * 128, gi0:gi0 + gsz], deps=dep_gu)
            for ib in range(cdiv(gsz, 128)):
                wd_evac.setdefault(g, []).append(dma(
                    wdt_g[par][:, ib * D:(ib + 1) * D],
                    wdT_d[gi0 + ib * 128: gi0 + ib * 128 + 128, :], deps=dep_d))

        def gate_up(g):
            gi0 = g * IGF
            gsz = min(IGF, I - gi0)
            par = g % 2
            assert 2 * gsz <= 512 and gsz == IGF
            for pi in range(T // 2):
                t0, t1 = 2 * pi, 2 * pi + 1
                ig, iu = 2 * (pi % 2), 2 * (pi % 2) + 1
                pg, pu = ps_mm[ig], ps_mm[iu]
                tok = None
                for ti, t in ((0, t0), (1, t1)):
                    xr = xq_ready[t] if isinstance(xq_ready, list) else xq_ready
                    for fc in range(FC):
                        d = [ev_g[g][fc]]
                        if fc == 0:
                            d += [xr]
                            if ti == 0:
                                d += [mm_free[ig], mm_free[iu]]
                        tok = p.emit("tensor", lambda e, f=fc, ppg=pg, pp=par,
                                     gz=gsz, tt=t, o=ti * gsz: e.matmul(
                            ppg[:, o:o + gz],
                            xq_fm[:, tt * D + f * 128: tt * D + f * 128 + 128],
                            wgt_g[pp][:, f * IGF: f * IGF + gz],
                            start=(f == 0), stop=(f == FC - 1)),
                            deps=d, inc="PE")
                for ti, t in ((0, t0), (1, t1)):
                    for fc in range(FC):
                        tok = p.emit("tensor", lambda e, f=fc, ppu=pu, pp=par,
                                     gz=gsz, tt=t, o=ti * gsz: e.matmul(
                            ppu[:, o:o + gz],
                            xq_fm[:, tt * D + f * 128: tt * D + f * 128 + 128],
                            wut_g[pp][:, f * IGF: f * IGF + gz],
                            start=(f == 0), stop=(f == FC - 1)),
                            deps=[ev_u[g][fc]], inc="PE")
                if t1 == T - 1:
                    wread_done[g] = tok
                t_sil = p.emit("scalar", lambda e, tt=t0, ppg=pg, gz=gsz,
                               hb=h_buf[g % 2]: e.activation(
                    out=hb[:, tt * IGF: tt * IGF + 2 * gz],
                    in_=ppg[:, 0:2 * gz], func=AF.Silu),
                    deps=[tok, hT_done.get((g - 2, t0)),
                          hT_done.get((g - 2, t1)), hq_done.get(g - 2)],
                    inc="AC", chain=False)
                t_mul = p.emit("vector", lambda e, tt=t0, ppu=pu, gz=gsz,
                               hb=h_buf[g % 2]: e.tensor_tensor(
                    out=hb[:, tt * IGF: tt * IGF + 2 * gz],
                    in0=hb[:, tt * IGF: tt * IGF + 2 * gz],
                    in1=ppu[:, 0:2 * gz], op=ALU.mult),
                    deps=[t_sil], inc="DV", chain=False)
                silu_mul_done[(g, t0)] = t_mul
                silu_mul_done[(g, t1)] = t_mul
                mm_free[ig] = t_sil
                mm_free[iu] = t_mul

        def quant_h(g):
            t_q = None
            pend = None
            for ci, off in enumerate(range(0, T * IGF, QW)):
                lt = min(T - 1, (off + QW - 1) // IGF)
                st = quant_a(h_buf[g % 2][:, off:off + QW],
                             h_buf[g % 2][:, off:off + QW], QW,
                             deps=[silu_mul_done[(g, lt)]])
                if pend is not None:
                    t_q = quant_b(pend[0])
                    hqc_done[(g, pend[1])] = t_q
                pend = (st, ci)
            t_q = quant_b(pend[0])
            hqc_done[(g, pend[1])] = t_q
            hq_done[g] = t_q

        def down(g):
            gi0 = g * IGF
            gsz = min(IGF, I - gi0)
            nib = cdiv(gsz, 128)
            par = g % 2
            for t in range(T):
                trp = t % 2
                tok = transpose_batch(
                    [h_buf[g % 2][:, t * IGF + ib * 128: t * IGF + ib * 128 + 128]
                     for ib in range(nib)], trp,
                    deps=[hqc_done[(g, (t * IGF) // QW)]])
                prev_dm = dmm_done.get((g, t - 2)) or dmm_done.get((g - 1, T - 2 + (t % 2)))
                ev = tr_evac_act(trp, hq_im[trp][:, 0:nib * 128], nib * 128,
                                 deps=[tok, prev_dm])
                hT_done[(g, t)] = ev
                dm_toks = [None, None]
                for hh in range(2):
                    psy = ps_y[hh]
                    for ib in range(nib):
                        dm_toks[hh] = p.emit("tensor", lambda e, b_=ib, h_=hh,
                                             py=psy, pr=trp, pp=par,
                                             hi=hq_im[trp]: e.matmul(
                            py[:], hi[:, b_ * 128:(b_ + 1) * 128],
                            wdt_g[pp][:, b_ * D + h_ * 512: b_ * D + h_ * 512 + 512],
                            start=(b_ == 0), stop=(b_ == nib - 1)),
                            deps=(([ev] + wd_evac[g] if hh == 0 else [])
                                  + [y_free[hh]]) if ib == 0 else (),
                            inc="PE")
                tok2 = dm_toks[1]
                dmm_done[(g, t)] = tok2
                ya = None
                for hh in range(2):
                    psy = ps_y[hh]
                    ya = p.emit("vector", lambda e, tt=t, h_=hh, py=psy: e.tensor_tensor(
                        out=x_res[:, tt * D + h_ * 512: tt * D + h_ * 512 + 512],
                        in0=x_res[:, tt * D + h_ * 512: tt * D + h_ * 512 + 512],
                        in1=py[:], op=ALU.add),
                        deps=[dm_toks[hh]], inc="DV", chain=False)
                    y_free[hh] = ya
                yacc_done[(g, t)] = ya
                if g == NGF - 1 and out_dram is not None:
                    out_toks[t] = dma(out_dram[t * 128:(t + 1) * 128, :],
                                      x_res[:, t * D:(t + 1) * D],
                                      deps=[ya], sem="St")

        FENCE_FFN = bool(globals().get("_FFN_FENCE", False))
        if wq_direct:
            wq_chains(0)
            wq_tails(0)
            if NGF > 1:
                wq_chains(1)
        else:
            load_group(0)
            if NGF > 1:
                load_group(1)
        for g in range(NGF):
            if FENCE_FFN:
                p.fence()
            gate_up(g)
            if FENCE_FFN:
                p.fence()
            if g >= 1:
                down(g - 1)
            if FENCE_FFN:
                p.fence()
            if wq_direct:
                if g + 1 < NGF:
                    wq_tails(g + 1)
                quant_h(g)
                if g + 2 < NGF:
                    wq_chains(g + 2)
            else:
                quant_h(g)
                if g + 1 < NGF:
                    load_group(g + 1)
        if FENCE_FFN:
            p.fence()
        down(NGF - 1)
        grp_done = [st_grp.get(g, []) for g in range(NGF)]
        return out_toks, yacc_done[(NGF - 1, T - 1)], grp_done

    if DBG and (DBGMASK & 2):
        xres_h = nc.dram_tensor("xres_dbg", [128, T * D], F32, kind="ExternalOutput").ap()
        st_h = nc.dram_tensor("st_dbg", [128, 4 * T], F32, kind="ExternalOutput").ap()
    if DBG and (DBGMASK & 1):
        stpre = sq_es.enter_context(nc.sbuf_tensor(
            "stpre", [128, 2 * T], F32, side="left"))
        stpre_h = nc.dram_tensor("stpre_dbg", [128, 2 * T], F32, kind="ExternalOutput").ap()
    x_toks, xq1_ready = lnq_phase(x_h, x_res, xq_fm, None, None,
                                  lnq_dbg=[stpre] if (DBG and (DBGMASK & 1)) else None)
    import os
    wqd = os.environ.get("WQ_DELAY")
    if wqd:
        if wqd == "full":
            p.emit("sync", lambda e: e.nop(), deps=[xq1_ready], inc=None)
        else:
            sname, val = wqd.split(":")
            p.emit("sync", lambda e: e.nop(), deps=[(sname, int(val))], inc=None)
    if DBG and (DBGMASK & 1):
        dma(stpre_h, stpre[:], deps=[xq1_ready], sem="St")
    if DBG and (DBGMASK & 2):
        dma(xres_h, x_res[:], deps=[x_toks, xq1_ready], sem="St")
        dma(st_h[:, 0:T], st_sum[:], deps=[xq1_ready], sem="St")
        dma(st_h[:, T:2*T], st_sq[:], deps=[xq1_ready], sem="St")
        dma(st_h[:, 2*T:3*T], st_rstd[:], deps=[xq1_ready], sem="St")
        dma(st_h[:, 3*T:4*T], st_nbias[:], deps=[xq1_ready], sem="St")
    if DBG and (DBGMASK & 2):
        xqd_h = nc.dram_tensor("xq_dbg", [128, T * D], BF16, kind="ExternalOutput").ap()
        dma(xqd_h, xq_fm[:], deps=[xq1_ready], sem="St")
    hq_im = [lnb_ref[0][:, 0:512], lnb_ref[0][:, 512:1024]]

    _, ffn1_done, slab_done = ffn_phase(
        x_res, xq_fm, h_buf, wgt_g, wut_g, wdt_g, hq_im,
        xq1_ready, None, out_dram=None,
        wq_direct=(wraw, wtm, stat_prev[0]), IGF=IG1)

    p.fence()
    rs1c.close()

    lnx2_es = contextlib.ExitStack()
    lnx2_fm = lnx2_es.enter_context(
        nc.sbuf_tensor("lnx2_fm", [128, FC * S], BF16, side="left"))
    sq_ln2 = lnx2_es.enter_context(
        nc.sbuf_tensor("sq_ln2", [128, D], F32, side="left"))
    stat_dump[0] = sq_ln2[:]
    spill_toks = []
    lnb = lnb_ref[0]
    st_last = None
    for t in range(T):
        st_last = ln_stats([ffn1_done], x_res[:, t * D:(t + 1) * D], t)
        spill_toks.append(dma(x1_d[t * 128:(t + 1) * 128, :],
                              x_res[:, t * D:(t + 1) * D],
                              deps=[ffn1_done], sem="St"))
    t_nb2 = ln_finalize([st_last])
    ln2_last = None
    ln2_tr = {0: None, 1: None}
    for t in range(T):
        half = t % 2
        lslice = lnb[:, half * D:(half + 1) * D]
        aff = p.emit("scalar", lambda e, tt=t, ls=lslice: e.activation(
            out=ls, in_=x_res[:, tt * D:(tt + 1) * D], func=AF.Identity,
            scale=st_rstd[:, tt:tt + 1], bias=st_nbias[:, tt:tt + 1]),
            deps=[t_nb2, ln2_tr[half]], inc="AC", chain=False)
        par = t % 2
        tok = transpose_batch(
            [lslice[:, fc * 128:(fc + 1) * 128] for fc in range(FC)],
            par, deps=[aff])
        ln2_tr[half] = tok
        ev = None
        for fc in range(FC):
            ev = p.emit("vector", lambda e, f=fc, tt=t, pr=par: e.tensor_copy(
                out=lnx2_fm[:, f * S + tt * 128: f * S + tt * 128 + 128],
                in_=ps_tr[pr][:, f * 128:(f + 1) * 128]),
                deps=[tok] if fc == 0 else (), inc="DV", chain=False)
        tr_free[par] = ev
        ln2_last = ev
    lnx2_ready = ln2_last
    if DBG and (DBGMASK & 8):
        lnx2d_h = nc.dram_tensor("lnx2_dbg", [128, FC * S], BF16, kind="ExternalOutput").ap()
        rstd_h = nc.dram_tensor("rstd_dbg", [128, T], F32, kind="ExternalOutput").ap()
        dma(lnx2d_h, lnx2_fm[:], deps=[lnx2_ready], sem="St")
        dma(rstd_h, st_rstd[:], deps=[lnx2_ready], sem="St")

    p.fence()
    rs1b.close()
    rs1.close()
    rsA = contextlib.ExitStack()
    k_fm = sbR(rsA, "k_fm", [128, FC * S], BF16)
    v_tm = sbR(rsA, "v_tm", [128, T * D], BF16)
    q_fm = sbR(rsA, "q_fm", [128, FC * S], BF16)
    rsAw = contextlib.ExitStack()
    wq_raw = sbR(rsAw, "wq_raw", [128, FC * 512], F32)
    wq_bf = sbR(rsAw, "wq_bf", [128, FC * 512], BF16)

    qkv_ev = None
    wq_read = {}
    wcast_done = {}
    for nck in range(6):
        t = []
        for fc in range(FC):
            t.append(dma(wq_raw[:, fc * 512:(fc + 1) * 512],
                    wqkv_h[fc * 128:(fc + 1) * 128, nck * 512:(nck + 1) * 512],
                    deps=[wcast_done.get(nck - 1)]))
        t_cast = p.emit("vector", lambda e: e.tensor_copy(
            out=wq_bf[:], in_=wq_raw[:]),
            deps=[t, wq_read.get(nck - 1)], inc="DV", chain=False)
        wcast_done[nck] = t_cast
        if nck < 4:
            dst = q_fm if nck < 2 else k_fm
            bias_off = 0 if nck < 2 else FC
            for nsub in range(4):
                dq = (nck % 2) * 4 + nsub
                for tc in range(4):
                    mi = (nsub * 4 + tc) % 4
                    pp = ps_mm[mi]
                    tok = None
                    for fc in range(FC):
                        tok = p.emit("tensor", lambda e, f=fc, ns=nsub, tcc=tc, ppp=pp: e.matmul(
                            ppp[:],
                            wq_bf[:, f * 512 + ns * 128: f * 512 + ns * 128 + 128],
                            lnx2_fm[:, f * S + tcc * 512: f * S + (tcc + 1) * 512],
                            start=(f == 0), stop=(f == FC - 1)),
                            deps=[t_cast, lnx2_ready, mm_free[mi]] if fc == 0 else (),
                            inc="PE")
                    wq_read[nck] = tok
                    qkv_ev = p.emit("scalar", lambda e, d_=dst, dqq=dq, tcc=tc,
                                    bo=bias_off, ppp=pp: e.activation(
                        out=d_[:, dqq * S + tcc * 512: dqq * S + (tcc + 1) * 512],
                        in_=ppp[:], func=AF.Identity,
                        bias=bqkv_t[:, bo + dqq: bo + dqq + 1]),
                        deps=[tok, bias_ready], inc="AC", chain=False)
                    mm_free[mi] = qkv_ev
        else:
            dvoff = (nck - 4) * 512
            for t_ in range(T):
                mi = t_ % 4
                pp = ps_mm[mi]
                tok = None
                for fc in range(FC):
                    tok = p.emit("tensor", lambda e, f=fc, tt=t_, ppp=pp: e.matmul(
                        ppp[:],
                        lnx2_fm[:, f * S + tt * 128: f * S + tt * 128 + 128],
                        wq_bf[:, f * 512:(f + 1) * 512],
                        start=(f == 0), stop=(f == FC - 1)),
                        deps=[t_cast, lnx2_ready, mm_free[mi]] if fc == 0 else (),
                        inc="PE")
                wq_read[nck] = tok
                qkv_ev = p.emit("scalar", lambda e, tt=t_, dv=dvoff, ppp=pp: e.activation(
                    out=v_tm[:, tt * D + dv: tt * D + dv + 512], in_=ppp[:],
                    func=AF.Identity), deps=[tok], inc="AC", chain=False)
                mm_free[mi] = qkv_ev
    qkv_done = qkv_ev
    if DBG and (DBGMASK & 4):
        qd_h = nc.dram_tensor("q_dbg", [128, FC * S], BF16, kind="ExternalOutput").ap()
        kd_h = nc.dram_tensor("k_dbg", [128, FC * S], BF16, kind="ExternalOutput").ap()
        vd_h = nc.dram_tensor("v_dbg", [128, T * D], BF16, kind="ExternalOutput").ap()
        dma(qd_h, q_fm[:], deps=[qkv_done], sem="St")
        dma(kd_h, k_fm[:], deps=[qkv_done], sem="St")
        dma(vd_h, v_tm[:], deps=[qkv_done], sem="St")

    p.fence()
    rsAw.close()
    lnx2_es.close()
    sq_es.close()
    rsB = contextlib.ExitStack()
    wo_raw = sbR(rsB, "wo_raw", [128, FC * 256], F32)
    wo_bf = sbR(rsB, "wo_bf", [128, FC * D], BF16)
    s_sb = sbR(rsB, "s_sb", [128, S], F32)
    p_sb = sbR(rsB, "p_sb", [128, S], BF16)
    pT_sb = sbR(rsB, "pT_sb", [128, T * 512], BF16)
    ctx_sb = sbR(rsB, "ctx_sb", [128, FC * 512], BF16)
    o_sb = sbR(rsB, "o_sb", [128, FC * 512], BF16)
    x1_t = sbR(rsB, "x1_t", [128, D], F32)
    sm_m = sbR(rsB, "sm_m", [128, 8], F32)

    t_wo = None
    for rnd in range(4):
        t = []
        for fc in range(FC):
            t.append(dma(wo_raw[:, fc * 256:(fc + 1) * 256],
                    wout_h[fc * 128:(fc + 1) * 128, rnd * 256:(rnd + 1) * 256],
                    deps=[t_wo]))
        half = rnd // 2
        off = (rnd % 2) * 256
        t_wo = p.emit("vector", lambda e, h_=half, o_=off: e.tensor_copy(
            out=wo_bf[:].rearrange("p (h f w) -> p h f w", h=2, f=FC)[
                :, h_, :, o_:o_ + 256],
            in_=wo_raw[:].rearrange("p (f w) -> p f w", w=256)),
            deps=[t], inc="DV", chain=False)

    x2_toks = [None] * T
    x1_load = {}
    if DBG and (DBGMASK & 16):
        pd_h = nc.dram_tensor("p_dbg", [128, S], BF16, kind="ExternalOutput").ap()
        sd_h = nc.dram_tensor("s_dbg", [128, S], F32, kind="ExternalOutput").ap()
        smd_h = nc.dram_tensor("sm_dbg", [128, 8], F32, kind="ExternalOutput").ap()
        ctxd_h = nc.dram_tensor("ctx_dbg", [128, FC * 512], BF16, kind="ExternalOutput").ap()
        od_h = nc.dram_tensor("o_dbg", [128, FC * 512], BF16, kind="ExternalOutput").ap()
    for qg in range(4):
        pT_evs = []
        tr_toks = {}
        for k in range(4):
            qt = qg * 4 + k
            ev_toks = []
            for kc in range(4):
                pp = ps_mm[kc]
                tok = None
                for fc in range(FC):
                    tok = p.emit("tensor", lambda e, f=fc, q_=qt, kcc=kc, ppp=pp: e.matmul(
                        ppp[:],
                        q_fm[:, f * S + q_ * 128: f * S + q_ * 128 + 128],
                        k_fm[:, f * S + kcc * 512: f * S + (kcc + 1) * 512],
                        start=(f == 0), stop=(f == FC - 1)),
                        deps=[qkv_done, mm_free[kc]] if fc == 0 else (), inc="PE")
                ev = p.emit("vector", lambda e, kcc=kc, ppp=pp: e.tensor_copy(
                    out=s_sb[:, kcc * 512:(kcc + 1) * 512], in_=ppp[:]),
                    deps=[tok, (("AC", p.cnt["AC"]) if (qg or k) else None)],
                    inc="DV", chain=False)
                ev_toks.append(ev)
                mm_free[kc] = ev
            t_m = p.emit("vector", lambda e: e.tensor_reduce(
                out=sm_m[:, 0:1], in_=s_sb[:], axis=AX.X, op=ALU.max, negate=True),
                deps=ev_toks, inc="DV")
            t_m = p.emit("vector", lambda e: e.tensor_scalar(
                out=sm_m[:, 0:1], in0=sm_m[:, 0:1], scalar1=1.0 / 32.0,
                scalar2=None, op0=ALU.mult), inc="DV")
            t_e = p.emit("scalar", lambda e, kk=k: e.activation(
                out=p_sb[:], in_=s_sb[:], func=AF.Exp, scale=1.0 / 32.0,
                bias=sm_m[:, 0:1], accum_out=sm_m[:, 1 + kk:2 + kk]),
                deps=[t_m, tr_toks.get(k - 1)], inc="AC", chain=False)
            t_r = p.emit("scalar", lambda e, kk=k: e.activation(
                out=sm_m[:, 5:6], in_=sm_m[:, 1 + kk:2 + kk], func=AF.Ln), inc="AC")
            t_r = p.emit("scalar", lambda e: e.activation(
                out=sm_m[:, 5:6], in_=sm_m[:, 5:6], func=AF.Exp, scale=-1.0),
                inc="AC")
            t_pn = p.emit("vector", lambda e: e.tensor_scalar(
                out=p_sb[:], in0=p_sb[:], scalar1=sm_m[:, 5:6], scalar2=None,
                op0=ALU.mult), deps=[t_r], inc="DV", chain=False)
            if DBG and (DBGMASK & 16) and qg == 0 and k == 0:
                dma(pd_h, p_sb[:], deps=[t_pn], sem="St")
                dma(sd_h, s_sb[:], deps=[t_pn], sem="St")
                dma(smd_h, sm_m[:], deps=[t_pn], sem="St")
            for half in range(2):
                par = half
                tok = transpose_batch(
                    [p_sb[:, (half * 8 + kt) * 128:(half * 8 + kt) * 128 + 128]
                     for kt in range(8)], par, deps=[t_pn])
                tr_toks[k] = tok
                ev = tr_evac("vector", par, lambda e, h_=half, kk=k, pr=par: e.tensor_copy(
                    out=pT_sb[:].rearrange("p (kt w) -> p kt w", w=512)[
                        :, h_ * 8:(h_ + 1) * 8, kk * 128:(kk + 1) * 128],
                    in_=ps_tr[pr][:].rearrange("p (b w) -> p b w", w=128)),
                    deps=[tok])
                pT_evs.append(ev)
        av_ev = None
        for dc in range(FC):
            mi = dc % 4
            pp = ps_mm[mi]
            tok = None
            for kt in range(T):
                tok = p.emit("tensor", lambda e, ktt=kt, d_=dc, ppp=pp: e.matmul(
                    ppp[:],
                    v_tm[:, ktt * D + d_ * 128: ktt * D + d_ * 128 + 128],
                    pT_sb[:, ktt * 512:(ktt + 1) * 512],
                    start=(ktt == 0), stop=(ktt == T - 1)),
                    deps=(pT_evs + [mm_free[mi]]) if kt == 0 else (), inc="PE")
            av_ev = p.emit("scalar", lambda e, d_=dc, ppp=pp: e.activation(
                out=ctx_sb[:, d_ * 512:(d_ + 1) * 512], in_=ppp[:],
                func=AF.Identity,
                bias=bqkv_t[:, 2 * FC + d_: 2 * FC + d_ + 1]),
                deps=[tok], inc="AC", chain=False)
            mm_free[mi] = av_ev
        oj_ev = None
        for oc in range(FC):
            mi = oc % 4
            pp = ps_mm[mi]
            woff = (oc // 4) * FC * 512 + (oc % 4) * 128
            tok = None
            for dc in range(FC):
                tok = p.emit("tensor", lambda e, d_=dc, ppp=pp, wo=woff: e.matmul(
                    ppp[:], wo_bf[:, d_ * 512 + wo: d_ * 512 + wo + 128],
                    ctx_sb[:, d_ * 512:(d_ + 1) * 512],
                    start=(d_ == 0), stop=(d_ == FC - 1)),
                    deps=[av_ev, t_wo, mm_free[mi]] if dc == 0 else (), inc="PE")
            oj_ev = p.emit("scalar", lambda e, o_=oc, ppp=pp: e.activation(
                out=o_sb[:, o_ * 512:(o_ + 1) * 512], in_=ppp[:],
                func=AF.Identity, bias=bout_t[:, o_: o_ + 1]),
                deps=[tok], inc="AC", chain=False)
            mm_free[mi] = oj_ev
        if DBG and (DBGMASK & 16) and qg == 0:
            dma(ctxd_h, ctx_sb[:], deps=[av_ev], sem="St")
            dma(od_h, o_sb[:], deps=[oj_ev], sem="St")
        for k in range(4):
            t_abs = qg * 4 + k
            x1_load[t_abs] = dma(
                x1_t[:],
                x1_d[t_abs * 128:(t_abs + 1) * 128, :],
                deps=[spill_toks[t_abs], x2_toks[t_abs - 1] if t_abs >= 1 else None])
            par = k % 2
            tok = transpose_batch(
                [o_sb[:, oc * 512 + k * 128: oc * 512 + k * 128 + 128]
                 for oc in range(FC)], par, deps=[oj_ev])
            t_add = tr_evac("vector", par, lambda e, pr=par: e.tensor_tensor(
                out=x1_t[:], in0=x1_t[:], in1=ps_tr[pr][:], op=ALU.add),
                deps=[tok, x1_load[t_abs]])
            x2_toks[t_abs] = dma(x2_d[t_abs * 128:(t_abs + 1) * 128, :],
                                 x1_t[:], deps=[t_add], sem="St")

    p.fence()
    rsB.close()
    rsA.close()
    rs2 = contextlib.ExitStack()
    x_res2 = sbR(rs2, "x_res2", [128, T * D], F32)
    lnb_ref[0] = sbR(rs2, "lnb2", [128, 2 * D], BF16)
    xq_fm2 = sbR(rs2, "xq_fm2", [128, T * D], BF16)
    h_buf2 = [sbR(rs2, "h2_0", [128, T * IG1], BF16),
              sbR(rs2, "h2_1", [128, T * IG1], BF16)]
    wgt_g2 = [sbR(rs2, "wgt2_0", [128, FC * IG1], BF16),
              sbR(rs2, "wgt2_1", [128, FC * IG1], BF16)]
    wut_g2 = [sbR(rs2, "wut2_0", [128, FC * IG1], BF16),
              sbR(rs2, "wut2_1", [128, FC * IG1], BF16)]
    wdt_g2 = [sbR(rs2, "wdt2_0", [128, 2 * D], BF16),
              sbR(rs2, "wdt2_1", [128, 2 * D], BF16)]
    hq_im2 = [lnb_ref[0][:, 0:512], lnb_ref[0][:, 512:1024]]
    sq_sc2 = sbR(rs2, "sq_sc2", [128, D], F32)
    stat_dump[0] = sq_sc2[:]

    _, xq2_ready = lnq_phase(x2_d, x_res2, xq_fm2, x2_toks, None)
    out_toks, _, _ = ffn_phase(x_res2, xq_fm2, h_buf2, wgt_g2, wut_g2, wdt_g2,
                               hq_im2, xq2_ready, slab_done, out_dram=y_h,
                               IGF=IG1)

    p.emit("sync", lambda e: e.nop(), deps=[st_fence_toks()], inc=None)

    p.check()
    with nc.Block() as block:
        p.realize(block)
    rs2.close()
    es.close()
    return nc


_NC_CACHE = {}


def kernel(L0_attn_x, L0_attn_W_qkv, L0_attn_b_qkv, L0_attn_W_out, L0_attn_b_out,
           L0_ffn_w_gate, L0_ffn_w_up, L0_ffn_w_down):
    B, S, D = L0_attn_x.shape
    key = (S, D)
    if key not in _NC_CACHE:
        _NC_CACHE[key] = build_nc(S=S, D=D, I=L0_ffn_w_gate.shape[0])
    nc = _NC_CACHE[key]
    wqkv = np.ascontiguousarray(L0_attn_W_qkv, dtype=np.float32)
    bqkv = np.ascontiguousarray(L0_attn_b_qkv, dtype=np.float32)
    wout = np.ascontiguousarray(L0_attn_W_out, dtype=np.float32)
    bout = np.ascontiguousarray(L0_attn_b_out, dtype=np.float32)
    wg = np.ascontiguousarray(L0_ffn_w_gate, dtype=np.float32)
    wu = np.ascontiguousarray(L0_ffn_w_up, dtype=np.float32)
    wd = np.ascontiguousarray(L0_ffn_w_down, dtype=np.float32)
    in_maps = []
    for i in range(B):
        in_maps.append({
            "x": np.ascontiguousarray(L0_attn_x[i], dtype=np.float32),
            "wqkv": wqkv, "bqkv": bqkv, "wout": wout, "bout": bout,
            "wg": wg, "wu": wu, "wd": wd,
        })
    res = run_bass_kernel_spmd(nc, in_maps, core_ids=list(range(B)))
    out = np.stack([np.asarray(r["y"]) for r in res.results], axis=0)
    return out.astype(np.float32)
